# version 27 (final)
# baseline (speedup 1.0000x reference)
import numpy as np
from contextlib import ExitStack

import concourse.bass as bass
import concourse.mybir as mybir
from concourse.bass_utils import run_bass_kernel_spmd

F32 = mybir.dt.float32
BF16 = mybir.dt.bfloat16
AF = mybir.ActivationFunctionType
ALU = mybir.AluOpType
AX = mybir.AxisListType.X

NCORES = 8
LN_EPS = 1e-5

REAL = dict(D=1024, H=16, E=8, F=4096, TC=2048, FCH=1024)


class Sem:
    def __init__(self, h):
        self.h = h
        self.n = 0


def build_nc(cfg, dbg=False):
    D, H, E, F, TC, FCH = cfg["D"], cfg["H"], cfg["E"], cfg["F"], cfg["TC"], cfg["FCH"]
    DB = D // 128
    SUB = TC // 128
    TT = NCORES * TC
    TILES = TT // 128
    FB = FCH // 128
    NCH = E * F // FCH
    DH = 64
    PAN = D // 2
    NJC = max(1, D // 512)
    GRP_SUBS = min(4, SUB)
    NGRP = SUB // GRP_SUBS
    GW = GRP_SUBS * 128
    NBN = max(1, D // 512)
    FPB = F // 128
    SCALE = 1.0 / float(np.sqrt(DH))
    CAP = ((TC * 5 // 8) + 127) // 128 * 128
    NSG = CAP // 128
    SGW = []
    _rem = CAP
    while _rem > 0:
        SGW.append(min(512, _rem))
        _rem -= min(512, _rem)
    NQ = F // FCH
    BIGP = 1.0e6

    nc = bass.Bass("TRN2", target_bir_lowering=False, debug=False,
                   dynamic_dma_scratch_size=16384)

    x_shard = nc.dram_tensor("x_shard", [TC, D], F32, kind="ExternalInput").ap()
    x_full = nc.dram_tensor("x_full", [TT, D], F32, kind="ExternalInput").ap()
    wqt_d = nc.dram_tensor("WqT", [D, D], F32, kind="ExternalInput").ap()
    wk_d = nc.dram_tensor("Wk", [D, D], F32, kind="ExternalInput").ap()
    wv_d = nc.dram_tensor("Wv", [D, D], F32, kind="ExternalInput").ap()
    wot_d = nc.dram_tensor("WoT", [D, D], F32, kind="ExternalInput").ap()
    wrt_d = nc.dram_tensor("WrT", [D, E], F32, kind="ExternalInput").ap()
    br_d = nc.dram_tensor("br", [1, E], F32, kind="ExternalInput").ap()
    w1t_d = nc.dram_tensor("W1T", [E, D, F], F32, kind="ExternalInput").ap()
    b1r_d = nc.dram_tensor("b1r", [E, 128, FPB], F32, kind="ExternalInput").ap()
    w2t_d = nc.dram_tensor("W2T", [E, F, D], F32, kind="ExternalInput").ap()
    b2_d = nc.dram_tensor("b2", [E, D], F32, kind="ExternalInput").ap()
    gam_d = nc.dram_tensor("gamma", [1, D], F32, kind="ExternalInput").ap()
    bet_d = nc.dram_tensor("beta", [1, D], F32, kind="ExternalInput").ap()
    y_d = nc.dram_tensor("y", [TC, D], F32, kind="ExternalOutput").ap()
    xn_scr = nc.dram_tensor("xn_scr", [TT, D], BF16).ap()
    xn_own = nc.dram_tensor("xn_own", [TC, D], F32).ap()
    xnT_dram = nc.dram_tensor("xnT_scr", [128, TC * DB], BF16).ap()
    acc_dram = nc.dram_tensor("acc_dram", [TC + 128, D], F32).ap()
    stage_x = nc.dram_tensor("stage_x", [E * CAP, D + 4], F32).ap()
    if dbg:
        dbg_xnT = nc.dram_tensor("dbg_xnT", [128, TC * (D // 128)], BF16, kind="ExternalOutput").ap()
        dbg_xnT_e = nc.dram_tensor("dbg_xnT_e", [128, TC * (D // 128)], BF16, kind="ExternalOutput").ap()
        dbg_R = nc.dram_tensor("dbg_R", [128, TC // 128 * E], F32, kind="ExternalOutput").ap()
        dbg_C = nc.dram_tensor("dbg_C", [128, 2048], BF16, kind="ExternalOutput").ap()
        dbg_kv = nc.dram_tensor("dbg_kv", [128, 512], BF16, kind="ExternalOutput").ap()
        dbg_acc = nc.dram_tensor("dbg_acc", [128, 2 * D], BF16, kind="ExternalOutput").ap()
        dbg_g = nc.dram_tensor("dbg_g", [128, D], F32, kind="ExternalOutput").ap()

    prog = {k: [] for k in ("sync", "gpsimd", "vector", "scalar", "tensor")}

    with ExitStack() as ctx:
        EE = ctx.enter_context
        xnTs = [EE(nc.sbuf_tensor(f"xnTs{i}", [128, DB * 128], BF16))
                for i in range(2)]
        xgT = EE(nc.sbuf_tensor("xgT", [128, DB * CAP], BF16))
        G_sb = xgT if CAP >= D else EE(nc.sbuf_tensor("G_sb", [128, DB * D], BF16))
        acc_slots = [EE(nc.sbuf_tensor(f"acc_slots{i}", [128, NSG * D], BF16))
                     for i in range(2)]
        xg = [EE(nc.sbuf_tensor(f"xg{i}", [128, D + 4], F32)) for i in range(2)]
        selS = EE(nc.sbuf_tensor("selS", [128, SUB * E], F32))
        sel1_sb = EE(nc.sbuf_tensor("sel1_sb", [128, SUB * E], F32))
        w12_sb = EE(nc.sbuf_tensor("w12_sb", [128, SUB * 2], F32))
        posg = EE(nc.sbuf_tensor("posg", [128, SUB * E], F32))
        pr_sb = EE(nc.sbuf_tensor("pr_sb", [128, SUB * 2], F32))
        pr_i = EE(nc.sbuf_tensor("pr_i", [128, SUB * 2], mybir.dt.int32))
        selS_bf = EE(nc.sbuf_tensor("selS_bf", [128, SUB * E], BF16))
        cnt_sb = EE(nc.sbuf_tensor("cnt_sb", [128, SUB * E], F32))
        L_bf = EE(nc.sbuf_tensor("L_bf", [128, 128], BF16))
        onescol_bf = EE(nc.sbuf_tensor("onescol_bf", [128, 1], BF16))
        ones1_f = EE(nc.sbuf_tensor("ones1_f", [1, 128], F32))
        tot_row = EE(nc.sbuf_tensor("tot_row", [1, SUB * E], F32))
        offsA = EE(nc.sbuf_tensor("offsA", [1, SUB * E], F32))
        offsB = EE(nc.sbuf_tensor("offsB", [1, SUB * E], F32))
        base_row = EE(nc.sbuf_tensor("base_row", [1, SUB * E], F32))
        tki_f = EE(nc.sbuf_tensor("tki_f", [128, SUB], F32))
        tok_all = EE(nc.sbuf_tensor("tok_all", [128, 2 * NSG], mybir.dt.int32))
        rg_all = EE(nc.sbuf_tensor("rg_all", [128, 2 * NSG], F32))
        WB = DB * FCH + FB * D
        wbuf = [EE(nc.sbuf_tensor(f"wbuf{i}", [128, max(WB, 2 * DB * D)], BF16)) for i in range(2)]
        scr = EE(nc.sbuf_tensor("scr", [128, DB * D], BF16))
        qT_off = 0
        aoT_off = D
        xb = [EE(nc.sbuf_tensor(f"xb{i}", [128, D], F32)) for i in range(2)]
        xm = [EE(nc.sbuf_tensor(f"xm{i}", [128, D], F32)) for i in range(2)]
        xnf = [EE(nc.sbuf_tensor(f"xnf{i}", [128, D], F32)) for i in range(2)]
        xnb = [EE(nc.sbuf_tensor(f"xnb{i}", [128, D], BF16)) for i in range(2)]
        g_sb0 = EE(nc.sbuf_tensor("g_sb0", [128, D], F32))
        gb = EE(nc.sbuf_tensor("gb", [128, 2 * D], BF16))
        st_sb = [EE(nc.sbuf_tensor(f"st{i}", [128, 6 * NBN], F32)) for i in range(4)]
        ag_sb = [EE(nc.sbuf_tensor(f"ag{i}", [128, 2], F32)) for i in range(4)]
        rstd = [EE(nc.sbuf_tensor(f"rstd{i}", [128, 1], F32)) for i in range(4)]
        nmu = [EE(nc.sbuf_tensor(f"nmu{i}", [128, 1], F32)) for i in range(4)]
        nmur = [EE(nc.sbuf_tensor(f"nmur{i}", [128, 1], F32)) for i in range(4)]
        stu = EE(nc.sbuf_tensor("stu", [128, SUB * 6 * NBN], F32))
        agu = EE(nc.sbuf_tensor("agu", [128, SUB * 2], F32))
        rstdu = EE(nc.sbuf_tensor("rstdu", [128, SUB], F32))
        nmuu = EE(nc.sbuf_tensor("nmuu", [128, SUB], F32))
        nmuru = EE(nc.sbuf_tensor("nmuru", [128, SUB], F32))
        R_sb = EE(nc.sbuf_tensor("R_sb", [128, SUB * E], F32))
        RT4 = [EE(nc.sbuf_tensor(f"RT4_{i}", [E, 512], BF16)) for i in range(2)]
        lg_all = EE(nc.sbuf_tensor("lg_all", [128, SUB * E], F32))
        iota_big = EE(nc.sbuf_tensor("iota_big", [128, SUB * E], F32))
        mxa = EE(nc.sbuf_tensor("mxa", [128, 4 * SUB], F32))
        wc = EE(nc.sbuf_tensor("wc", [128, 2 * SUB], F32))
        bias_sb = EE(nc.sbuf_tensor("bias_sb", [128, D], BF16))
        eps_col = EE(nc.sbuf_tensor("eps_col", [128, 1], F32))
        id_f32 = EE(nc.sbuf_tensor("id_f32", [128, 128], F32))
        ones_bf = EE(nc.sbuf_tensor("ones_bf", [1, 128], BF16))
        wrt_bf = EE(nc.sbuf_tensor("wrt_bf", [128, DB * E], BF16))
        br_bf = EE(nc.sbuf_tensor("br_bf", [1, E], BF16))
        b1r_sb = EE(nc.sbuf_tensor("b1r_sb", [128, E * FPB], F32))
        b2_bf = EE(nc.sbuf_tensor("b2_bf", [E, D], BF16))
        B_bf0 = EE(nc.sbuf_tensor("B_bf0", [128, DB * DH], BF16))
        B_bf = [B_bf0, B_bf0]
        kv_bf = EE(nc.sbuf_tensor("kv_bf", [128, (H // 2) * DH], BF16))
        pb = [EE(nc.psum_tensor(f"pb{i}", [128, 512], F32)) for i in range(8)]

        sT = Sem(EE(nc.semaphore("sT")))
        sV = Sem(EE(nc.semaphore("sV")))
        sS = Sem(EE(nc.semaphore("sS")))
        sP = Sem(EE(nc.semaphore("sP")))
        sSmS = Sem(EE(nc.semaphore("sSmS")))
        sSmG = Sem(EE(nc.semaphore("sSmG")))
        sxb = [Sem(EE(nc.semaphore(f"sxb{i}"))) for i in range(2)]
        sxn = [Sem(EE(nc.semaphore(f"sxn{i}"))) for i in range(2)]
        sStr = [Sem(EE(nc.semaphore(f"sStr{i}"))) for i in range(2)]
        swb = [Sem(EE(nc.semaphore(f"swb{i}"))) for i in range(2)]
        sout = [Sem(EE(nc.semaphore(f"sout{i}"))) for i in range(2)]
        sDg = Sem(EE(nc.semaphore("sDg")))
        sxo2 = [Sem(EE(nc.semaphore(f"sxo{i}"))) for i in range(2)]
        sxT = [Sem(EE(nc.semaphore(f"sxT{i}"))) for i in range(2)]
        sRd = Sem(EE(nc.semaphore("sRd")))
        sac = [Sem(EE(nc.semaphore(f"sac{i}"))) for i in range(2)]
        szf = Sem(EE(nc.semaphore("szf")))
        ssc2 = [Sem(EE(nc.semaphore(f"ssc{i}"))) for i in range(2)]
        ssg = [Sem(EE(nc.semaphore(f"ssg{i}"))) for i in range(2)]
        sxg = [Sem(EE(nc.semaphore(f"sxg{i}"))) for i in range(2)]
        sst = [Sem(EE(nc.semaphore(f"sst{i}"))) for i in range(2)]
        szb = Sem(EE(nc.semaphore("szb")))
        sza = Sem(EE(nc.semaphore("sza")))
        stk = Sem(EE(nc.semaphore("stk")))
        srg = Sem(EE(nc.semaphore("srg")))
        block = EE(nc.Block())

        xnTs3 = [t_[:, :].rearrange("p (b t2) -> p b t2", b=DB) for t_ in xnTs]
        xgT3 = xgT[:, :].rearrange("p (b t) -> p b t", b=DB)

        def op(eng, fn, waits=(), inc=None, inck=1):
            prog[eng].append((fn, tuple(waits), inc, inck))
            if inc is not None:
                inc.n += inck
            return inc.n if inc is not None else None

        def vo(fn, waits=()):
            return op("vector", fn, waits=[(sV, sV.n)] + list(waits), inc=sV)

        def so(fn, waits=()):
            return op("scalar", fn, waits=[(sS, sS.n)] + list(waits), inc=sS)

        def to(fn, waits=(), last=True):
            return op("tensor", fn, waits=waits, inc=(sT if last else None))

        op("gpsimd", lambda g: g.memset(id_f32[:, :], 0.0), inc=sP)
        op("gpsimd", lambda g: g.affine_select(
            out=id_f32[:, :], in_=id_f32[:, :], compare_op=ALU.not_equal,
            fill=1.0, base=0, pattern=[[-1, 128]], channel_multiplier=1),
            waits=[(sP, sP.n)], inc=sP)
        op("gpsimd", lambda g: g.memset(ones_bf[:, :], 1.0), inc=sP)
        op("gpsimd", lambda g: g.iota(
            iota_big[:, :], pattern=[[0, SUB], [-1, E]], base=E, channel_multiplier=0,
            allow_small_or_imprecise_dtypes=True), inc=sP)
        op("gpsimd", lambda g: g.memset(eps_col[:, :], LN_EPS), inc=sP)
        op("gpsimd", lambda g: g.memset(L_bf[:, :], 0.0), inc=sP)
        op("gpsimd", lambda g: g.affine_select(
            out=L_bf[:, :], in_=L_bf[:, :], compare_op=ALU.is_gt,
            fill=1.0, base=0, pattern=[[-1, 128]], channel_multiplier=1),
            waits=[(sP, sP.n)], inc=sP)
        op("gpsimd", lambda g: g.memset(onescol_bf[:, :], 1.0), inc=sP)
        op("gpsimd", lambda g: g.memset(ones1_f[:, :], 1.0), inc=sP)
        op("gpsimd", lambda g: g.iota(
            tki_f[:, :], pattern=[[128, SUB]], base=0, channel_multiplier=1,
            allow_small_or_imprecise_dtypes=True), inc=sP)
        op("gpsimd", lambda g: g.iota(
            base_row[:, :].rearrange("o (s e) -> o s e", e=E),
            pattern=[[0, SUB], [CAP, E]], base=0, channel_multiplier=0,
            allow_small_or_imprecise_dtypes=True), inc=sP)
        op("gpsimd", lambda g: g.memset(xg[0][:, :], 0.0), inc=sP)
        op("gpsimd", lambda g: g.memset(xg[0][:, D:D + 1], float(TC)),
           waits=[(sP, sP.n)], inc=sP)
        op("gpsimd", lambda g: g.memset(rg_all[:, :], 0.0), inc=sP)
        p0 = sP.n
        for i in range(E * CAP // 128):
            op("sync", lambda s, i=i: s.dma_start(
                out=stage_x[i * 128:(i + 1) * 128, :], in_=xg[0][:, :]),
                waits=[(sP, p0)] if i == 0 else [], inc=szf, inck=16)
        zf_all = szf.n

        for b in range(DB):
            op("gpsimd", lambda g, b=b: g.dma_start(
                out=wrt_bf[:, b * E:(b + 1) * E],
                in_=wrt_d[b * 128:(b + 1) * 128, :]), inc=sSmG, inck=16)
        op("gpsimd", lambda g: g.dma_start(out=br_bf[:, :], in_=br_d), inc=sSmG, inck=16)
        op("gpsimd", lambda g: g.dma_start(out=b2_bf[:, :], in_=b2_d), inc=sSmG, inck=16)
        op("gpsimd", lambda g: g.dma_start(out=scr[0:1, 0:D], in_=gam_d), inc=sSmG, inck=16)
        op("gpsimd", lambda g: g.dma_start(out=scr[0:1, D:2 * D], in_=bet_d), inc=sSmG, inck=16)
        g_small_sem, g_small = sSmG, sSmG.n
        for b in range(DB):
            op("gpsimd", lambda g, b=b: g.dma_start(
                out=wbuf[1][:, b * D:(b + 1) * D],
                in_=wk_d[b * 128:(b + 1) * 128, :]), inc=swb[1], inck=16)
            op("gpsimd", lambda g, b=b: g.dma_start(
                out=wbuf[1][:, DB * D + b * D:DB * D + (b + 1) * D],
                in_=wv_d[b * 128:(b + 1) * 128, :]), inc=swb[1], inck=16)
        g_kvw = swb[1].n

        def chunk_load(ch, wait_t):
            e, q = ch // (F // FCH), ch % (F // FCH)
            par = ch % 2
            for b in range(DB):
                op("gpsimd", lambda g, e=e, q=q, par=par, b=b: g.dma_start(
                    out=wbuf[par][:, b * FCH:(b + 1) * FCH],
                    in_=w1t_d[e][b * 128:(b + 1) * 128, q * FCH:(q + 1) * FCH]),
                    waits=([(sT, wait_t)] if b == 0 else []), inc=swb[par], inck=16)
            for fbb in range(FB):
                op("gpsimd", lambda g, e=e, q=q, par=par, fbb=fbb: g.dma_start(
                    out=wbuf[par][:, DB * FCH + fbb * D:DB * FCH + (fbb + 1) * D],
                    in_=w2t_d[e][q * FCH + fbb * 128:q * FCH + (fbb + 1) * 128, :]),
                    inc=swb[par], inck=16)
            return swb[par].n

        ch_ready = {0: chunk_load(0, 0)}

        op("gpsimd", lambda e: e.dma_start(
            out=b1r_sb[:, :].rearrange("p (e f) -> p e f", e=E),
            in_=b1r_d.rearrange("e p f -> p e f")), inc=sSmS, inck=16)

        for jc in range(NJC):
            to(lambda t, jc=jc: t.matmul(pb[2 + jc][:, :], ones_bf[:, :],
                                         scr[0:1, jc * 512:(jc + 1) * 512],
                                         start=True, stop=True),
               waits=[(sP, p0), (sSmG, g_small)])
            to(lambda t, jc=jc: t.matmul(pb[2 + NJC + jc][:, :], ones_bf[:, :],
                                         scr[0:1, D + jc * 512:D + (jc + 1) * 512],
                                         start=True, stop=True))
        t_gb = sT.n
        for jc in range(NJC):
            vo(lambda v, jc=jc: v.tensor_copy(gb[:, jc * 512:(jc + 1) * 512], pb[2 + jc][:, :]),
               [(sT, t_gb)])
            vo(lambda v, jc=jc: v.tensor_copy(gb[:, D + jc * 512:D + (jc + 1) * 512],
                                              pb[2 + NJC + jc][:, :]))
        v_gb = sV.n

        ag_ln = [0, 0, 0, 0]
        nmu_guard = [0, 0, 0, 0]

        def ln_stats(par, src, waits_first):
            for c in range(NBN):
                vo(lambda v, c=c, par=par: v.bn_stats(st_sb[par][:, c * 6:(c + 1) * 6],
                                                      src[:, c * 512:(c + 1) * 512]),
                   waits_first if c == 0 else [])
            vo(lambda v, par=par: v.bn_aggr(ag_sb[par][:, :], st_sb[par][:, :]),
               [(sS, ag_ln[par])])
            vo(lambda v, par=par: v.tensor_scalar(nmu[par][:, :], ag_sb[par][:, 0:1],
                                                  -1.0, None, op0=ALU.mult),
               [(sS, nmu_guard[par])])
            va = sV.n
            so(lambda s, par=par: s.activation(rstd[par][:, :], ag_sb[par][:, 1:2],
                                               AF.Ln, bias=eps_col[:, 0:1]), [(sV, va), (sP, p0)])
            ag_ln[par] = sS.n
            so(lambda s, par=par: s.activation(rstd[par][:, :], rstd[par][:, :],
                                               AF.Exp, scale=-0.5))
            so(lambda s, par=par: s.activation(nmur[par][:, :], rstd[par][:, :],
                                               AF.Copy, scale=nmu[par][:, 0:1]))
            nmu_guard[par] = sS.n
            return sS.n

        xb_free = [0, 0]
        t_lgdone = [0, 0]
        xT_store = [0, 0]
        xnf_free_t = [0, 0]
        ac_cnt = [0, 0]
        xo_cnt = [0, 0]
        s_evac_prev = 0
        v_lgfree = 0
        for s_i in range(SUB):
            par = s_i % 2
            op("sync", lambda e, s_i=s_i, par=par: e.dma_start(
                out=xb[par][:, :], in_=x_shard[s_i * 128:(s_i + 1) * 128, :]),
                waits=[(sS, xb_free[par])], inc=sxb[par], inck=16)
            lx = sxb[par].n
            s_rdy = ln_stats(par, xb[par], [(sxb[par], lx)])
            so(lambda s, par=par: s.activation(
                xnf[par][:, :], xb[par][:, :], AF.Identity,
                scale=rstd[par][:, 0:1], bias=nmur[par][:, 0:1]),
               [(sT, xnf_free_t[par]), (sxo2[par], xo_cnt[par]),
                (sac[par], ac_cnt[par])])
            xb_free[par] = sS.n
            s_xn = sS.n
            op("sync", lambda e, s_i=s_i, par=par: e.dma_start(
                out=xn_own[s_i * 128:(s_i + 1) * 128, :], in_=xnf[par][:, :]),
                waits=[(sS, s_xn)], inc=sxo2[par], inck=16)
            xo_cnt[par] = sxo2[par].n
            op("sync", lambda e, s_i=s_i, par=par: e.dma_start(
                out=acc_dram[s_i * 128:(s_i + 1) * 128, :], in_=xnf[par][:, :]),
                waits=[(sS, s_xn)], inc=sac[par], inck=16)
            ac_cnt[par] = sac[par].n
            for b in range(DB):
                to(lambda t, b=b, par=par: t.transpose(
                    pb[b // 4][:, (b % 4) * 128:(b % 4 + 1) * 128],
                    xnf[par][:, b * 128:(b + 1) * 128], id_f32[:, :]),
                   waits=([(sS, s_xn), (sS, s_evac_prev)] +
                          ([(sP, p0)] if s_i == 0 else [])) if b == 0 else [],
                   last=(b == DB - 1))
            xnf_free_t[par] = sT.n
            t_tr = sT.n
            nb0 = min(4, DB)
            so(lambda s, par=par, nb0=nb0: s.copy(
                xnTs3[par][:, 0:nb0, 0:128],
                pb[0][:, 0:nb0 * 128].rearrange("p (b t) -> p b t", b=nb0)),
               [(sT, t_tr), (sT, t_lgdone[par]), (sxT[par], xT_store[par])])
            if DB > 4:
                so(lambda s, par=par: s.copy(
                    xnTs3[par][:, 4:DB, 0:128],
                    pb[1][:, 0:(DB - 4) * 128].rearrange("p (b t) -> p b t", b=DB - 4)))
            s_evac_prev = sS.n
            s_xnT = sS.n
            op("sync", lambda e, s_i=s_i, par=par: e.dma_start(
                out=xnT_dram[:, s_i * DB * 128:(s_i + 1) * DB * 128],
                in_=xnTs[par][:, :]),
                waits=[(sS, s_xnT)], inc=sxT[par], inck=16)
            xT_store[par] = sxT[par].n
            for b in range(DB):
                op("tensor", lambda t, b=b, par=par: t.matmul(
                    pb[2][:, 0:E], xnTs3[par][:, b, 0:128],
                    wrt_bf[:, b * E:(b + 1) * E], start=(b == 0), stop=False),
                    waits=[(sS, s_xnT), (sSmG, g_small), (sV, v_gb if s_i == 0 else v_lgfree)]
                    if b == 0 else [])
            to(lambda t: t.matmul(pb[2][:, 0:E], ones_bf[:, :], br_bf[:, :],
                                  start=False, stop=True))
            t_lg = sT.n
            t_lgdone[par] = sT.n
            vo(lambda v, s_i=s_i: v.tensor_copy(lg_all[:, s_i * E:(s_i + 1) * E],
                                                pb[2][:, 0:E]), [(sT, t_lg)])
            v_lgfree = sV.n

        rt2a, rt3a = posg, cnt_sb
        lg3 = lg_all[:, :].rearrange("p (s e) -> p s e", e=E)
        rt2a3 = rt2a[:, :].rearrange("p (s e) -> p s e", e=E)
        rt3a3 = rt3a[:, :].rearrange("p (s e) -> p s e", e=E)

        def mxv(i):
            return mxa[:, i * SUB:(i + 1) * SUB].rearrange("p (s o) -> p s o", o=1)
        vo(lambda v: v.reduce_max(mxv(0)[:, :, :], lg3, axis=AX))
        vo(lambda v: v.tensor_tensor(rt2a3, lg3, mxv(0).broadcast_to((128, SUB, E)),
                                     op=ALU.is_ge), [(sP, p0)])
        vo(lambda v: v.tensor_mul(rt2a[:, :], rt2a[:, :], iota_big[:, :]))
        vo(lambda v: v.reduce_max(mxv(1)[:, :, :], rt2a3, axis=AX))
        vo(lambda v: v.tensor_tensor(rt2a3, rt2a3, mxv(1).broadcast_to((128, SUB, E)),
                                     op=ALU.is_ge))
        vo(lambda v: v.tensor_copy(sel1_sb[:, :], rt2a[:, :]))
        vo(lambda v: v.scalar_tensor_tensor(rt3a[:, :], rt2a[:, :], -1e30, lg_all[:, :],
                                            op0=ALU.mult, op1=ALU.add))
        vo(lambda v: v.reduce_max(mxv(2)[:, :, :], rt3a3, axis=AX))
        vo(lambda v: v.tensor_tensor(rt3a3, rt3a3, mxv(2).broadcast_to((128, SUB, E)),
                                     op=ALU.is_ge))
        vo(lambda v: v.tensor_mul(rt3a[:, :], rt3a[:, :], iota_big[:, :]))
        vo(lambda v: v.reduce_max(mxv(3)[:, :, :], rt3a3, axis=AX))
        vo(lambda v: v.tensor_tensor(rt3a3, rt3a3, mxv(3).broadcast_to((128, SUB, E)),
                                     op=ALU.is_ge))
        vo(lambda v: v.tensor_add(selS[:, :], rt2a[:, :], rt3a[:, :]))
        vo(lambda v: v.tensor_sub(wc[:, 0:SUB], mxa[:, 2 * SUB:3 * SUB], mxa[:, 0:SUB]))
        v_d = sV.n
        so(lambda s: s.activation(wc[:, SUB:2 * SUB], wc[:, 0:SUB], AF.Exp), [(sV, v_d)])
        s_e = sS.n
        vo(lambda v: v.tensor_scalar(wc[:, SUB:2 * SUB], wc[:, SUB:2 * SUB], 1.0, None,
                                     op0=ALU.add), [(sS, s_e)])
        vo(lambda v: v.reciprocal(wc[:, 0:SUB], wc[:, SUB:2 * SUB]))
        vo(lambda v: v.tensor_scalar(wc[:, SUB:2 * SUB], wc[:, 0:SUB], -1.0, -1.0,
                                     op0=ALU.mult, op1=ALU.subtract))
        w12v = w12_sb[:, :].rearrange("p (s r) -> p s r", r=2)
        w1v = wc[:, 0:SUB].rearrange("p (s o) -> p s o", o=1)
        w2v = wc[:, SUB:2 * SUB].rearrange("p (s o) -> p s o", o=1)
        vo(lambda v: v.tensor_copy(w12v[:, :, 0:1], w1v[:, :, :]))
        vo(lambda v: v.tensor_copy(w12v[:, :, 1:2], w2v[:, :, :]))
        vo(lambda v: v.tensor_tensor(rt2a3, rt2a3, w1v.broadcast_to((128, SUB, E)),
                                     op=ALU.mult))
        vo(lambda v: v.tensor_tensor(rt3a3, rt3a3, w2v.broadcast_to((128, SUB, E)),
                                     op=ALU.mult))
        vo(lambda v: v.tensor_add(R_sb[:, :], rt2a[:, :], rt3a[:, :]))
        v_R = sV.n

        s_rt4_used = [0, 0]
        t_rt4_read = [0, 0]
        s_bias_prev = 0
        szb_prev = 0
        for r in range((SUB + 3) // 4):
            rp = r % 2
            ks = list(range(4 * r, min(4 * r + 4, SUB)))
            for j, s_i in enumerate(ks):
                to(lambda t, s_i=s_i, j=j: t.transpose(
                    pb[6][0:E, j * 128:(j + 1) * 128],
                    R_sb[:, s_i * E:(s_i + 1) * E], id_f32[:, :]),
                   waits=([(sV, v_R), (sS, s_rt4_used[rp])] if j == 0 else []),
                   last=(j == len(ks) - 1))
            t_rt = sT.n
            so(lambda s, rp=rp, n=len(ks): s.copy(RT4[rp][0:E, 0:n * 128],
                                                  pb[6][0:E, 0:n * 128]),
               [(sT, t_rt), (sT, t_rt4_read[rp])])
            s_rt4 = sS.n
            s_rt4_used[rp] = sS.n
            for j, s_i in enumerate(ks):
                for jc in range(NJC):
                    to(lambda t, j=j, jc=jc, rp=rp: t.matmul(
                        pb[2 + jc][:, :], RT4[rp][0:E, j * 128:(j + 1) * 128],
                        b2_bf[0:E, jc * 512:(jc + 1) * 512],
                        start=True, stop=True),
                       waits=(([(sS, s_rt4), (sSmG, g_small)] if j == 0 else []) +
                              [(sS, s_bias_prev)]) if jc == 0 else [])
                t_bias = sT.n
                for jc in range(NJC):
                    so(lambda s, jc=jc: s.copy(bias_sb[:, jc * 512:(jc + 1) * 512],
                                               pb[2 + jc][:, :]),
                       [(sT, t_bias)] + ([(szb, szb_prev)] if jc == 0 else []))
                s_bias_prev = sS.n
                op("gpsimd", lambda g, s_i=s_i: g.dma_start(
                    out=acc_dram[s_i * 128:(s_i + 1) * 128, :], in_=bias_sb[:, :],
                    accum_op=ALU.add),
                    waits=[(sS, s_bias_prev), (sac[s_i % 2], ac_cnt[s_i % 2])],
                    inc=szb, inck=16)
                szb_prev = szb.n
            t_rt4_read[rp] = sT.n
        s_bias_done = sS.n
        szb_all = szb.n

        xnb_free = [0, 0]
        for t_i in range(TILES):
            par = t_i % 2
            op("sync", lambda e, t_i=t_i, par=par: e.dma_start(
                out=xb[par][:, :], in_=x_full[t_i * 128:(t_i + 1) * 128, :]),
                waits=[(sS, xb_free[par])], inc=sxb[par], inck=16)
            lx = sxb[par].n
            s_rdy = ln_stats(par, xb[par], [(sxb[par], lx)])
            so(lambda s, par=par: s.activation(
                xnb[par][:, :], xb[par][:, :], AF.Identity,
                scale=rstd[par][:, 0:1], bias=nmur[par][:, 0:1]),
               [(sT, xnb_free[par]), (sStr[par], 16 * (t_i // 2))])
            xb_free[par] = sS.n
            s_xn = sS.n
            op("gpsimd", lambda g, t_i=t_i, par=par: g.dma_start(
                out=xn_scr[t_i * 128:(t_i + 1) * 128, :], in_=xnb[par][:, :]),
                waits=[(sS, s_xn)], inc=sStr[par], inck=16)
            for ib in range(DB):
                w = []
                if ib == 0:
                    w = [(sS, s_xn)]
                    if t_i == 0:
                        w.append((sS, s_bias_done))
                to(lambda t, ib=ib, par=par, t_i=t_i: t.matmul(
                    pb[ib][:, 0:PAN], xnb[par][:, ib * 128:(ib + 1) * 128],
                    xnb[par][:, 0:PAN], start=(t_i == 0), stop=(t_i == TILES - 1)),
                   waits=w, last=(ib == DB - 1))
            xnb_free[par] = sT.n
        t_pan0 = sT.n
        for ib in range(DB):
            so(lambda s, ib=ib: s.copy(scr[:, ib * D:ib * D + PAN], pb[ib][:, 0:PAN]),
               [(sT, t_pan0)] if ib == 0 else [])
        s_pan0 = sS.n

        vo(lambda v: v.tensor_copy(selS_bf[:, :], selS[:, :]))
        v_selS = sV.n
        for s_i in range(SUB):
            to(lambda t, s_i=s_i: t.matmul(
                pb[4][:, s_i * E:(s_i + 1) * E], L_bf[:, :],
                selS_bf[:, s_i * E:(s_i + 1) * E], start=True, stop=True),
               waits=[(sV, v_selS), (sP, p0), (sS, s_pan0)] if s_i == 0 else [])
        to(lambda t: t.matmul(pb[5][0:1, 0:SUB * E], onescol_bf[:, :],
                              selS_bf[:, :], start=True, stop=True))
        t_cnt = sT.n
        vo(lambda v: v.tensor_copy(cnt_sb[:, :], pb[4][:, 0:SUB * E]), [(sT, t_cnt)])
        vo(lambda v: v.tensor_copy(tot_row[:, :], pb[5][0:1, 0:SUB * E]))
        vo(lambda v: v.memset(offsA[:, :], 0.0))
        vo(lambda v: v.tensor_copy(offsA[:, E:SUB * E], tot_row[:, 0:(SUB - 1) * E]))
        cur, oth = offsA, offsB
        sh = E
        while sh < SUB * E:
            vo(lambda v, cur=cur, oth=oth, sh=sh: v.tensor_copy(oth[:, 0:sh], cur[:, 0:sh]))
            vo(lambda v, cur=cur, oth=oth, sh=sh: v.tensor_add(
                oth[:, sh:SUB * E], cur[:, sh:SUB * E], cur[:, 0:SUB * E - sh]))
            cur, oth = oth, cur
            sh *= 2
        vo(lambda v, cur=cur: v.tensor_add(cur[:, :], cur[:, :], base_row[:, :]),
           [(sP, p0)])
        v_offs = sV.n
        to(lambda t, cur=cur: t.matmul(pb[5][:, 0:SUB * E], ones1_f[:, :], cur[:, :],
                                       start=True, stop=True),
           waits=[(sV, v_offs)])
        t_offs = sT.n
        vo(lambda v: v.tensor_scalar(cnt_sb[:, :], cnt_sb[:, :], -1.0, None,
                                     op0=ALU.add))
        vo(lambda v: v.tensor_add(posg[:, :], cnt_sb[:, :], pb[5][:, 0:SUB * E]),
           [(sT, t_offs)])
        pr3 = pr_sb[:, :].rearrange("p (s r) -> p s r", r=2)
        vo(lambda v: v.tensor_mul(selS[:, :], selS[:, :], posg[:, :]))
        vo(lambda v: v.tensor_mul(sel1_sb[:, :], sel1_sb[:, :], posg[:, :]))
        s13 = sel1_sb[:, :].rearrange("p (s e) -> p s e", e=E)
        sS3 = selS[:, :].rearrange("p (s e) -> p s e", e=E)
        vo(lambda v: v.reduce_sum(pr3[:, :, 0:1], s13, axis=AX))
        vo(lambda v: v.reduce_sum(pr3[:, :, 1:2], sS3, axis=AX))
        vo(lambda v: v.tensor_sub(pr_sb[:, :].rearrange("p (s r) -> p s r", r=2)[:, :, 1],
                                  pr_sb[:, :].rearrange("p (s r) -> p s r", r=2)[:, :, 1],
                                  pr_sb[:, :].rearrange("p (s r) -> p s r", r=2)[:, :, 0]))
        vo(lambda v: v.tensor_copy(pr_i[:, :], pr_sb[:, :]))
        v_posi = sV.n

        sc_guard = [0, 0]
        for s_i in range(SUB):
            for rk in range(2):
                p2 = (2 * s_i + rk) % 2
                op("gpsimd", lambda e, s_i=s_i, p2=p2: e.dma_start(
                    out=xg[p2][:, 0:D], in_=xn_own[s_i * 128:(s_i + 1) * 128, :]),
                    waits=[(szf, zf_all), (ssc2[p2], sc_guard[p2]), (sV, v_posi)] +
                          ([(sxo2[0], xo_cnt[0]), (sxo2[1], xo_cnt[1])]
                           if s_i == 0 else []),
                    inc=ssg[p2], inck=16)
                lxs = ssg[p2].n
                vo(lambda v, s_i=s_i, p2=p2: v.tensor_copy(
                    xg[p2][:, D:D + 1], tki_f[:, s_i:s_i + 1]), [(ssg[p2], lxs)])
                vo(lambda v, s_i=s_i, rk=rk, p2=p2: v.tensor_copy(
                    xg[p2][:, D + 1:D + 2], w12_sb[:, 2 * s_i + rk:2 * s_i + rk + 1]))
                v_xgw = sV.n
                op("gpsimd", lambda g, s_i=s_i, rk=rk, p2=p2: g.indirect_dma_start(
                    out=stage_x,
                    out_offset=bass.IndirectOffsetOnAxis(
                        pr_i[:, 2 * s_i + rk:2 * s_i + rk + 1], 0),
                    in_=xg[p2][:, 0:D + 2], in_offset=None,
                    bounds_check=None, oob_is_err=False),
                    waits=[(sV, v_xgw), (sP, p0)],
                    inc=ssc2[p2], inck=16)
                sc_guard[p2] = ssc2[p2].n
        sc_all = [ssc2[0].n, ssc2[1].n]

        xnb_ld_free = [t_pan0, t_pan0]
        for t_i in range(TILES):
            par = t_i % 2
            op("sync", lambda e, t_i=t_i, par=par: e.dma_start(
                out=xnb[par][:, :], in_=xn_scr[t_i * 128:(t_i + 1) * 128, :]),
                waits=[(sStr[0], sStr[0].n), (sStr[1], sStr[1].n),
                       (sT, xnb_ld_free[par])], inc=sxn[par], inck=16)
            lx = sxn[par].n
            for ib in range(DB):
                w = []
                if ib == 0:
                    w = [(sxn[par], lx)]
                    if t_i == 0:
                        w.append((sS, s_pan0))
                        w.append((sV, v_posi))
                to(lambda t, ib=ib, par=par, t_i=t_i: t.matmul(
                    pb[ib][:, 0:PAN], xnb[par][:, ib * 128:(ib + 1) * 128],
                    xnb[par][:, PAN:D], start=(t_i == 0), stop=(t_i == TILES - 1)),
                   waits=w, last=(ib == DB - 1))
            xnb_ld_free[par] = sT.n
        t_pan1 = sT.n
        for ib in range(DB):
            so(lambda s, ib=ib: s.copy(scr[:, ib * D + PAN:(ib + 1) * D], pb[ib][:, 0:PAN]),
               [(sT, t_pan1)] if ib == 0 else [])
        s_C = sS.n

        s_hfree = {0: 0, 1: 0}
        v_ofree = [v_posi, v_posi]
        t_xgfree = [0, 0]
        v_xgfree = [0, 0]
        s_trevac = {(6, 7): 0, (2, 3): s_bias_done}
        za_done = [0, 0]
        s_xgT_exp = {}

        def emit_pro(e_i, glo, ghi, banks=(6, 7)):
            ep = e_i % 2
            for g_i in range(glo, ghi):
                xp = g_i % 2
                op("sync", lambda e, e_i=e_i, g_i=g_i, xp=xp: e.dma_start(
                    out=xg[xp][:, :],
                    in_=stage_x[e_i * CAP + g_i * 128:e_i * CAP + (g_i + 1) * 128, :]),
                    waits=[(sT, t_xgfree[xp]), (sV, v_xgfree[xp])] +
                          ([(ssc2[0], sc_all[0]), (ssc2[1], sc_all[1])]
                           if (e_i == 0 and g_i < 2) else []),
                    inc=sxg[xp], inck=16)
                lxg = sxg[xp].n
                vo(lambda v, g_i=g_i, xp=xp, ep=ep: v.tensor_copy(
                    tok_all[:, ep * NSG + g_i:ep * NSG + g_i + 1], xg[xp][:, D:D + 1]),
                   [(sxg[xp], lxg)] +
                   ([(sza, za_done[ep])] if g_i == 0 else []))
                vo(lambda v, g_i=g_i, xp=xp, ep=ep: v.tensor_copy(
                    rg_all[:, ep * NSG + g_i:ep * NSG + g_i + 1], xg[xp][:, D + 1:D + 2]))
                v_xgfree[xp] = sV.n
                for b in range(DB):
                    to(lambda t, b=b, xp=xp, banks=banks: t.transpose(
                        pb[banks[b // 4]][:, (b % 4) * 128:(b % 4 + 1) * 128],
                        xg[xp][:, b * 128:(b + 1) * 128], id_f32[:, :]),
                       waits=([(sxg[xp], lxg), (sS, s_trevac[banks])] if b == 0 else []),
                       last=(b == DB - 1))
                t_xgfree[xp] = sT.n
                t_tr = sT.n
                nb0 = min(4, DB)
                so(lambda s, g_i=g_i, nb0=nb0, banks=banks: s.copy(
                    xgT3[:, 0:nb0, g_i * 128:(g_i + 1) * 128],
                    pb[banks[0]][:, 0:nb0 * 128].rearrange("p (b t) -> p b t", b=nb0)),
                   [(sT, t_tr)])
                if DB > 4:
                    so(lambda s, g_i=g_i, banks=banks: s.copy(
                        xgT3[:, 4:DB, g_i * 128:(g_i + 1) * 128],
                        pb[banks[1]][:, 0:(DB - 4) * 128].rearrange("p (b t) -> p b t", b=DB - 4)))
                s_trevac[banks] = sS.n
            s_xgT_exp[e_i] = sS.n

        s_Bfree = [s_C, s_C]
        for h in range(H):
            par = h % 2
            for ib in range(DB):
                for jb in range(DB):
                    w = []
                    if ib == 0 and jb == 0:
                        w = [(sS, s_C), (swb[1], g_kvw), (sS, s_Bfree[par])]
                    to(lambda t, ib=ib, jb=jb, par=par, h=h: t.matmul(
                        pb[par][:, ib * DH:(ib + 1) * DH],
                        scr[:, jb * D + ib * 128:jb * D + (ib + 1) * 128],
                        wbuf[1][:, jb * D + h * DH:jb * D + (h + 1) * DH],
                        start=(jb == 0), stop=(jb == DB - 1)),
                       waits=w, last=(ib == DB - 1 and jb == DB - 1))
            t_B = sT.n
            so(lambda s, par=par: s.copy(B_bf[par][:, :], pb[par][:, 0:DB * DH]), [(sT, t_B)])
            s_Bfree[par] = sS.n
            sBh = sS.n
            for ib in range(DB):
                to(lambda t, ib=ib, par=par, h=h: t.matmul(
                    pb[6][(h % 2) * 64:(h % 2) * 64 + 64, (h // 2) * DH:(h // 2 + 1) * DH],
                    wbuf[1][:, DB * D + ib * D + h * DH:DB * D + ib * D + (h + 1) * DH],
                    B_bf[par][:, ib * DH:(ib + 1) * DH],
                    start=(ib == 0), stop=(ib == DB - 1)),
                   waits=[(sS, sBh)] if ib == 0 else [], last=(ib == DB - 1))
            if h >= H - NSG:
                emit_pro(0, h - (H - NSG), h - (H - NSG) + 1, banks=(2, 3))
        t_kv = sT.n
        so(lambda s: s.activation(kv_bf[:, :], pb[6][:, 0:(H // 2) * DH], AF.Copy,
                                  scale=SCALE), [(sT, t_kv)])
        s_kve = sS.n
        s_trevac[(6, 7)] = s_kve

        if dbg:
            op("sync", lambda e: e.dma_start(out=dbg_R, in_=R_sb[:, :]),
               waits=[(sV, v_R)], inc=sDg, inck=16)
            op("sync", lambda e: e.dma_start(out=dbg_C, in_=scr[:, 0:2048]),
               waits=[(sS, s_C)], inc=sDg, inck=16)
            op("sync", lambda e: e.dma_start(out=dbg_kv, in_=kv_bf[:, 0:512]),
               waits=[(sS, sS.n)], inc=sDg, inck=16)
            dbg_pre_ffn = sDg.n

        if NCH > 1:
            ch_ready[1] = chunk_load(1, t_kv)

        for ch in range(NCH):
            e_i, q_i = ch // NQ, ch % NQ
            par = ch % 2
            ep = e_i % 2
            vg_grp = {}
            goff = 0
            for gw in SGW:
                for fb in range(FB):
                    hb_ = fb % 2
                    for b in range(DB):
                        w = []
                        if b == 0:
                            w = [(swb[par], ch_ready[ch]), (sS, s_hfree[hb_])]
                            if q_i == 0 and goff == 0 and fb == 0:
                                w.append((sS, s_xgT_exp[e_i]))
                        to(lambda t, fb=fb, b=b, goff=goff, gw=gw, par=par, hb_=hb_: t.matmul(
                            pb[hb_][:, 0:gw],
                            wbuf[par][:, b * FCH + fb * 128:b * FCH + (fb + 1) * 128],
                            xgT3[:, b, goff:goff + gw],
                            start=(b == 0), stop=(b == DB - 1)),
                           waits=w, last=(b == DB - 1))
                    t_h = sT.n
                    so(lambda s, fb=fb, e_i=e_i, q_i=q_i, gw=gw, hb_=hb_: s.activation(
                        scr[:, fb * 512:fb * 512 + gw], pb[hb_][:, 0:gw], AF.Relu,
                        bias=b1r_sb[:, e_i * FPB + q_i * FB + fb:e_i * FPB + q_i * FB + fb + 1]),
                       [(sT, t_h)] + ([(sSmS, 16)] if (ch == 0 and goff == 0 and fb == 0) else []))
                    s_hfree[hb_] = sS.n
                s_hT_all = sS.n
                for sl in range(gw // 128):
                    sg_i = goff // 128 + sl
                    spar = sg_i % 2
                    for jc in range(NJC):
                        bank = 2 + spar * 2 + jc
                        for fb in range(FB):
                            w = []
                            if fb == 0:
                                w = [(sS, s_hT_all), (sV, v_ofree[spar])]
                            to(lambda t, fb=fb, sl=sl, jc=jc, par=par, bank=bank: t.matmul(
                                pb[bank][:, :],
                                scr[:, fb * 512 + sl * 128:fb * 512 + (sl + 1) * 128],
                                wbuf[par][:, DB * FCH + fb * D + jc * 512:DB * FCH + fb * D + (jc + 1) * 512],
                                start=(fb == 0), stop=(fb == FB - 1)),
                               waits=w, last=(fb == FB - 1))
                    t_o = sT.n
                    for jc in range(NJC):
                        bank = 2 + spar * 2 + jc
                        if q_i == 0:
                            vo(lambda v, sg_i=sg_i, jc=jc, bank=bank, ep=ep: v.tensor_scalar(
                                acc_slots[ep][:, sg_i * D + jc * 512:sg_i * D + (jc + 1) * 512],
                                pb[bank][:, :],
                                rg_all[:, ep * NSG + sg_i:ep * NSG + sg_i + 1], None,
                                op0=ALU.mult),
                               [(sT, t_o)] +
                               ([(sza, za_done[ep])]
                                if (goff == 0 and sl == 0 and jc == 0) else []))
                        else:
                            vo(lambda v, sg_i=sg_i, jc=jc, bank=bank, ep=ep: v.scalar_tensor_tensor(
                                acc_slots[ep][:, sg_i * D + jc * 512:sg_i * D + (jc + 1) * 512],
                                pb[bank][:, :],
                                rg_all[:, ep * NSG + sg_i:ep * NSG + sg_i + 1],
                                acc_slots[ep][:, sg_i * D + jc * 512:sg_i * D + (jc + 1) * 512],
                                op0=ALU.mult, op1=ALU.add),
                               [(sT, t_o)])
                    v_ofree[spar] = sV.n
                    if q_i == NQ - 1:
                        vg_grp[sg_i] = sV.n
                if q_i == NQ - 1 and e_i + 1 < E:
                    emit_pro(e_i + 1, goff // 128, (goff + gw) // 128)
                goff += gw
            if ch + 2 < NCH:
                ch_ready[ch + 2] = chunk_load(ch + 2, sT.n)
            if ch == NCH - 2:
                p3par = NCH % 2
                for b in range(DB):
                    op("gpsimd", lambda g, b=b, p3par=p3par: g.dma_start(
                        out=wbuf[p3par][:, b * D:(b + 1) * D],
                        in_=wqt_d[b * 128:(b + 1) * 128, :]),
                        waits=([(sT, sT.n)] if b == 0 else []), inc=swb[p3par], inck=16)
                    op("gpsimd", lambda g, b=b, p3par=p3par: g.dma_start(
                        out=wbuf[p3par][:, DB * D + b * D:DB * D + (b + 1) * D],
                        in_=wot_d[b * 128:(b + 1) * 128, :]), inc=swb[p3par], inck=16)
                g_p3w = swb[p3par].n
            if q_i == NQ - 1:
                for g_i in range(NSG):
                    w = [(sza, sza.n), (sV, vg_grp[g_i])]
                    if g_i == 0:
                        if e_i == 0:
                            w += [(sac[0], sac[0].n), (sac[1], sac[1].n),
                                  (szb, szb_all)]
                    op("gpsimd", lambda g, g_i=g_i, ep=ep: g.indirect_dma_start(
                        out=acc_dram,
                        out_offset=bass.IndirectOffsetOnAxis(
                            tok_all[:, ep * NSG + g_i:ep * NSG + g_i + 1], 0),
                        in_=acc_slots[ep][:, g_i * D:(g_i + 1) * D], in_offset=None,
                        bounds_check=None, oob_is_err=False,
                        compute_op=ALU.add),
                        waits=w, inc=sza, inck=16)
                za_done[ep] = sza.n
        t_ffn = sT.n
        v_ffn = sV.n

        NPAIR = H // 2
        ppw = max(1, 8 // NJC)
        first_N = True
        s_N_prev = 0
        pair = 0
        while pair < NPAIR:
            wave = list(range(pair, min(pair + ppw, NPAIR)))
            for i, pr in enumerate(wave):
                for h2 in range(2):
                    for jc in range(NJC):
                        w = []
                        if first_N:
                            w = [(sS, s_hfree[0]), (sS, s_hfree[1]),
                                 (sV, v_ofree[0]), (sV, v_ofree[1]),
                                 (sS, s_trevac[(6, 7)]), (sS, s_trevac[(2, 3)]),
                                 (sS, s_kve), (swb[p3par], g_p3w)]
                            first_N = False
                        elif i == 0 and h2 == 0 and jc == 0 and s_N_prev:
                            w = [(sS, s_N_prev)]
                        to(lambda t, i=i, pr=pr, h2=h2, jc=jc: t.matmul(
                            pb[i * NJC + jc][h2 * 64:h2 * 64 + 64, :],
                            kv_bf[h2 * 64:h2 * 64 + 64, pr * DH:(pr + 1) * DH],
                            wbuf[p3par][h2 * 64:h2 * 64 + 64,
                                        DB * D + pr * D + jc * 512:
                                        DB * D + pr * D + (jc + 1) * 512],
                            start=True, stop=True),
                           waits=w,
                           last=(i == len(wave) - 1 and h2 == 1 and jc == NJC - 1))
            t_N = sT.n
            for i, pr in enumerate(wave):
                for jc in range(NJC):
                    so(lambda s, i=i, pr=pr, jc=jc: s.copy(
                        scr[:, pr * D + jc * 512:pr * D + (jc + 1) * 512],
                        pb[i * NJC + jc][:, :]),
                       [(sT, t_N)] if (i == 0 and jc == 0) else [])
            s_N_prev = sS.n
            pair += ppw
        s_N_all = sS.n

        v_Gevac_prev = v_ffn
        NH2 = H // 2
        for jc2 in range(NJC):
            for half in range(2):
                dbs = list(range(half * (DB // 2), (half + 1) * (DB // 2)))
                for i, db in enumerate(dbs):
                    for h2 in range(2):
                        for k in range(NH2):
                            h = 2 * k + h2
                            bank = i + h2 * 4
                            w = []
                            if k == 0 and h2 == 0 and i == 0:
                                w = [(sS, s_N_all), (sV, v_Gevac_prev),
                                     (sV, v_ofree[0]), (sV, v_ofree[1])]
                            to(lambda t, db=db, h=h, jc2=jc2, bank=bank, k=k: t.matmul(
                                pb[bank][:, :],
                                wbuf[p3par][(h % 2) * 64:(h % 2) * 64 + 64,
                                            (h // 2) * D + db * 128:(h // 2) * D + (db + 1) * 128],
                                scr[(h % 2) * 64:(h % 2) * 64 + 64,
                                    (h // 2) * D + jc2 * 512:(h // 2) * D + (jc2 + 1) * 512],
                                start=(k == 0), stop=(k == NH2 - 1)),
                               waits=w, last=(k == NH2 - 1))
                t_G = sT.n
                for i, db in enumerate(dbs):
                    so(lambda s, db=db, i=i, jc2=jc2: s.copy(
                        G_sb[:, db * D + jc2 * 512:db * D + (jc2 + 1) * 512],
                        pb[i + 4][:, :]),
                       [(sT, t_G)] if i == 0 else [])
                s_Ghalf = sS.n
                for i, db in enumerate(dbs):
                    vo(lambda v, db=db, i=i, jc2=jc2: v.tensor_tensor(
                        G_sb[:, db * D + jc2 * 512:db * D + (jc2 + 1) * 512],
                        pb[i][:, :],
                        G_sb[:, db * D + jc2 * 512:db * D + (jc2 + 1) * 512],
                        op=ALU.add),
                       [(sS, s_Ghalf)] if i == 0 else [])
                v_Gevac_prev = sV.n
        v_G_all = sV.n

        agu2 = agu[:, :].rearrange("p (s two) -> p s two", two=2)
        ub_free = [sV.n, sV.n]
        for s_i in range(SUB):
            par = s_i % 2
            op("sync", lambda e, s_i=s_i, par=par: e.dma_start(
                out=xb[par][:, :], in_=acc_dram[s_i * 128:(s_i + 1) * 128, :]),
                waits=[(sV, ub_free[par])] + ([(sza, sza.n)] if s_i == 0 else []),
                inc=sxb[par], inck=16)
            lu = sxb[par].n
            for c in range(NBN):
                vo(lambda v, c=c, s_i=s_i, par=par: v.bn_stats(
                    stu[:, (s_i * NBN + c) * 6:(s_i * NBN + c) * 6 + 6],
                    xb[par][:, c * 512:(c + 1) * 512]),
                   [(sxb[par], lu)] if c == 0 else [])
            vo(lambda v, s_i=s_i: v.bn_aggr(agu[:, s_i * 2:(s_i + 1) * 2],
                                            stu[:, s_i * NBN * 6:(s_i + 1) * NBN * 6]))
            ub_free[par] = sV.n
        vo(lambda v: v.tensor_scalar(nmuu[:, :], agu2[:, :, 0], -1.0, None, op0=ALU.mult))
        v_stats = sV.n
        so(lambda s: s.activation(rstdu[:, :], agu2[:, :, 1], AF.Ln, bias=eps_col[:, 0:1]),
           [(sV, v_stats)])
        so(lambda s: s.activation(rstdu[:, :], rstdu[:, :], AF.Exp, scale=-0.5))
        s_rstdu = sS.n
        vo(lambda v: v.tensor_mul(nmuru[:, :], rstdu[:, :], nmuu[:, :]), [(sS, s_rstdu)])
        v_nmuru = sV.n

        xb_free = [v_nmuru, v_nmuru]
        t_xq = [t_ffn, t_ffn]
        v_aw_p3 = [v_ffn, v_ffn]
        v_sfree = [v_ffn, v_ffn]
        g_guard = [v_ffn, v_ffn]
        zn_done = [0, 0]
        for s_i in range(SUB):
            par = s_i % 2
            op("sync", lambda e, s_i=s_i, par=par: e.dma_start(
                out=xb[par][:, :], in_=x_shard[s_i * 128:(s_i + 1) * 128, :]),
                waits=[(sV, xb_free[par])], inc=sxb[par], inck=16)
            lx = sxb[par].n
            uw = [(sS, zn_done[par]), (sV, v_aw_p3[par])]
            if s_i >= 2:
                uw.append((sout[par], 16 * (s_i // 2)))
            op("sync", lambda e, s_i=s_i, par=par: e.dma_start(
                out=xm[par][:, :], in_=acc_dram[s_i * 128:(s_i + 1) * 128, :]),
                waits=uw, inc=sxn[par], inck=16)
            lu2 = sxn[par].n
            so(lambda s, s_i=s_i, par=par: s.activation(
                xnf[par][:, :], xm[par][:, :], AF.Identity,
                scale=rstdu[:, s_i:s_i + 1], bias=nmuru[:, s_i:s_i + 1]),
               [(sV, v_nmuru), (sxn[par], lu2)])
            zn_done[par] = sS.n
            s_zn = sS.n
            vo(lambda v, par=par: v.tensor_mul(xnf[par][:, :], xnf[par][:, :],
                                               gb[:, 0:D]),
               [(sS, s_zn)] + ([(sV, v_gb)] if s_i == 0 else []))
            vo(lambda v, par=par: v.tensor_add(xnf[par][:, :], xnf[par][:, :],
                                               gb[:, D:2 * D]))
            v_z = sV.n
            so(lambda s, par=par: s.activation(g_sb0[:, :], xnf[par][:, :], AF.Silu),
               [(sV, v_z), (sV, g_guard[0])])
            s_g = sS.n
            if dbg and s_i == 0:
                op("sync", lambda e: e.dma_start(out=dbg_g, in_=g_sb0[:, :]),
                   waits=[(sS, s_g)], inc=sDg, inck=16)
            op("sync", lambda e, s_i=s_i, par=par: e.dma_start(
                out=xnTs[par][:, :],
                in_=xnT_dram[:, s_i * DB * 128:(s_i + 1) * DB * 128]),
                waits=[(sT, t_xq[par]), (sxT[par], xT_store[par])],
                inc=sxT[par], inck=16)
            l_xT = sxT[par].n
            pbq = 2 + 2 * par
            for jc in range(NJC):
                for b in range(DB):
                    w = []
                    if jc == 0 and b == 0:
                        w = [(sV, v_G_all), (sV, v_sfree[par]), (sxT[par], l_xT)]
                    to(lambda t, jc=jc, b=b, par=par, pbq=pbq: t.matmul(
                        pb[pbq + jc][:, :],
                        xnTs3[par][:, b, 0:128],
                        G_sb[:, b * D + jc * 512:b * D + (jc + 1) * 512],
                        start=(b == 0), stop=(b == DB - 1)),
                       waits=w, last=(jc == NJC - 1 and b == DB - 1))
            t_xq[par] = sT.n
            t_s = sT.n
            for jc in range(NJC):
                vo(lambda v, jc=jc, par=par, pbq=pbq: v.tensor_mul(
                    xm[par][:, jc * 512:(jc + 1) * 512], pb[pbq + jc][:, :],
                    g_sb0[:, jc * 512:(jc + 1) * 512]),
                   [(sT, t_s), (sS, s_g)] + ([(sDg, sDg.n)] if dbg else []))
            v_sfree[par] = sV.n
            g_guard[0] = sV.n
            fw = [(sxb[par], lx)]
            if s_i >= 2:
                fw.append((sout[par], 16 * (s_i // 2)))
            vo(lambda v, par=par: v.tensor_add(xm[par][:, :], xm[par][:, :], xb[par][:, :]), fw)
            xb_free[par] = sV.n
            v_fin = sV.n
            op("sync", lambda e, s_i=s_i, par=par: e.dma_start(
                out=y_d[s_i * 128:(s_i + 1) * 128, :], in_=xm[par][:, :]),
                waits=[(sV, v_fin)], inc=sout[par], inck=16)

        op("sync", lambda e: e.nop(),
           waits=[(sout[0], sout[0].n), (sout[1], sout[1].n)] +
                 ([(sDg, sDg.n)] if dbg else []))

        def emit(e, key):
            for fn, waits, inc, inck in prog[key]:
                for s, v in waits:
                    e.wait_ge(s.h, v)
                inst = fn(e)
                if inc is not None:
                    inst.then_inc(inc.h, inck)

        @block.sync
        def _(e):
            emit(e, "sync")

        @block.gpsimd
        def _(e):
            emit(e, "gpsimd")

        @block.vector
        def _(e):
            emit(e, "vector")

        @block.scalar
        def _(e):
            emit(e, "scalar")

        @block.tensor
        def _(e):
            emit(e, "tensor")

    return nc


def make_inputs(cfg, x2d, Wq, Wk, Wv, Wo, Wr, br, W1, b1, W2, b2, gamma, beta):
    D, E, F, TC = cfg["D"], cfg["E"], cfg["F"], cfg["TC"]
    com = {
        "x_full": np.ascontiguousarray(x2d),
        "WqT": np.ascontiguousarray(Wq.T),
        "Wk": np.ascontiguousarray(Wk),
        "Wv": np.ascontiguousarray(Wv),
        "WoT": np.ascontiguousarray(Wo.T),
        "WrT": np.ascontiguousarray(Wr.T),
        "br": np.ascontiguousarray(br.reshape(1, E)),
        "W1T": np.ascontiguousarray(W1.transpose(0, 2, 1)),
        "b1r": np.ascontiguousarray(b1.reshape(E, F // 128, 128).transpose(0, 2, 1)),
        "W2T": np.ascontiguousarray(W2.transpose(0, 2, 1)),
        "b2": np.ascontiguousarray(b2),
        "gamma": np.ascontiguousarray(gamma.reshape(1, D)),
        "beta": np.ascontiguousarray(beta.reshape(1, D)),
    }
    in_maps = []
    for c in range(NCORES):
        m = dict(com)
        m["x_shard"] = np.ascontiguousarray(x2d[c * TC:(c + 1) * TC])
        in_maps.append(m)
    return in_maps


def kernel(L0_attn_x, L0_attn_Wq, L0_attn_Wk, L0_attn_Wv, L0_attn_Wo,
           L0_ffn_Wr, L0_ffn_br, L0_ffn_W1, L0_ffn_b1, L0_ffn_W2, L0_ffn_b2,
           L0_ffn_gamma, L0_ffn_beta):
    cfg = REAL
    x = np.asarray(L0_attn_x, dtype=np.float32)
    B, S, D = x.shape
    x2d = x.reshape(S, D)
    nc = build_nc(cfg)
    in_maps = make_inputs(
        cfg, x2d,
        np.asarray(L0_attn_Wq, np.float32), np.asarray(L0_attn_Wk, np.float32),
        np.asarray(L0_attn_Wv, np.float32), np.asarray(L0_attn_Wo, np.float32),
        np.asarray(L0_ffn_Wr, np.float32), np.asarray(L0_ffn_br, np.float32),
        np.asarray(L0_ffn_W1, np.float32), np.asarray(L0_ffn_b1, np.float32),
        np.asarray(L0_ffn_W2, np.float32), np.asarray(L0_ffn_b2, np.float32),
        np.asarray(L0_ffn_gamma, np.float32), np.asarray(L0_ffn_beta, np.float32))
    res = run_bass_kernel_spmd(nc, in_maps, core_ids=list(range(NCORES)))
    out = np.concatenate([res.results[c]["y"] for c in range(NCORES)], axis=0)
    return out.reshape(B, S, D)


# version 28
# speedup vs baseline: 1.0345x; 1.0345x over previous
import numpy as np
from contextlib import ExitStack

import concourse.bass as bass
import concourse.mybir as mybir
from concourse.bass_utils import run_bass_kernel_spmd

F32 = mybir.dt.float32
BF16 = mybir.dt.bfloat16
AF = mybir.ActivationFunctionType
ALU = mybir.AluOpType
AX = mybir.AxisListType.X

NCORES = 8
LN_EPS = 1e-5

REAL = dict(D=1024, H=16, E=8, F=4096, TC=2048, FCH=1024)


class Sem:
    def __init__(self, h):
        self.h = h
        self.n = 0


def build_nc(cfg, dbg=False):
    D, H, E, F, TC, FCH = cfg["D"], cfg["H"], cfg["E"], cfg["F"], cfg["TC"], cfg["FCH"]
    DB = D // 128
    SUB = TC // 128
    TT = NCORES * TC
    TILES = TT // 128
    FB = FCH // 128
    NCH = E * F // FCH
    DH = 64
    PAN = D // 2
    NJC = max(1, D // 512)
    GRP_SUBS = min(4, SUB)
    NGRP = SUB // GRP_SUBS
    GW = GRP_SUBS * 128
    NBN = max(1, D // 512)
    FPB = F // 128
    SCALE = 1.0 / float(np.sqrt(DH))
    CAP = ((TC * 5 // 8) + 127) // 128 * 128
    NSG = CAP // 128
    SGW = []
    _rem = CAP
    while _rem > 0:
        SGW.append(min(512, _rem))
        _rem -= min(512, _rem)
    NQ = F // FCH
    BIGP = 1.0e6

    nc = bass.Bass("TRN2", target_bir_lowering=False, debug=False,
                   dynamic_dma_scratch_size=16384)

    x_shard = nc.dram_tensor("x_shard", [TC, D], F32, kind="ExternalInput").ap()
    x_full = nc.dram_tensor("x_full", [TT, D], F32, kind="ExternalInput").ap()
    wqt_d = nc.dram_tensor("WqT", [D, D], F32, kind="ExternalInput").ap()
    wk_d = nc.dram_tensor("Wk", [D, D], F32, kind="ExternalInput").ap()
    wv_d = nc.dram_tensor("Wv", [D, D], F32, kind="ExternalInput").ap()
    wot_d = nc.dram_tensor("WoT", [D, D], F32, kind="ExternalInput").ap()
    wrt_d = nc.dram_tensor("WrT", [D, E], F32, kind="ExternalInput").ap()
    br_d = nc.dram_tensor("br", [1, E], F32, kind="ExternalInput").ap()
    w1t_d = nc.dram_tensor("W1T", [E, D, F], F32, kind="ExternalInput").ap()
    b1r_d = nc.dram_tensor("b1r", [E, 128, FPB], F32, kind="ExternalInput").ap()
    w2t_d = nc.dram_tensor("W2T", [E, F, D], F32, kind="ExternalInput").ap()
    b2_d = nc.dram_tensor("b2", [E, D], F32, kind="ExternalInput").ap()
    gam_d = nc.dram_tensor("gamma", [1, D], F32, kind="ExternalInput").ap()
    bet_d = nc.dram_tensor("beta", [1, D], F32, kind="ExternalInput").ap()
    y_d = nc.dram_tensor("y", [TC, D], F32, kind="ExternalOutput").ap()
    xn_scr = nc.dram_tensor("xn_scr", [TT, D], BF16).ap()
    xn_own = nc.dram_tensor("xn_own", [TC, D], F32).ap()
    xnT_dram = nc.dram_tensor("xnT_scr", [128, TC * DB], BF16).ap()
    acc_dram = nc.dram_tensor("acc_dram", [TC + 128, D], F32).ap()
    stage_x = nc.dram_tensor("stage_x", [E * CAP, D + 4], F32).ap()
    if dbg:
        dbg_xnT = nc.dram_tensor("dbg_xnT", [128, TC * (D // 128)], BF16, kind="ExternalOutput").ap()
        dbg_xnT_e = nc.dram_tensor("dbg_xnT_e", [128, TC * (D // 128)], BF16, kind="ExternalOutput").ap()
        dbg_R = nc.dram_tensor("dbg_R", [128, TC // 128 * E], F32, kind="ExternalOutput").ap()
        dbg_C = nc.dram_tensor("dbg_C", [128, 2048], BF16, kind="ExternalOutput").ap()
        dbg_kv = nc.dram_tensor("dbg_kv", [128, 512], BF16, kind="ExternalOutput").ap()
        dbg_acc = nc.dram_tensor("dbg_acc", [128, 2 * D], BF16, kind="ExternalOutput").ap()
        dbg_g = nc.dram_tensor("dbg_g", [128, D], F32, kind="ExternalOutput").ap()

    prog = {k: [] for k in ("sync", "gpsimd", "vector", "scalar", "tensor")}

    with ExitStack() as ctx:
        EE = ctx.enter_context
        xnTs = [EE(nc.sbuf_tensor(f"xnTs{i}", [128, DB * 128], BF16))
                for i in range(2)]
        xgT = EE(nc.sbuf_tensor("xgT", [128, DB * CAP], BF16))
        G_sb = xgT if CAP >= D else EE(nc.sbuf_tensor("G_sb", [128, DB * D], BF16))
        acc_slots = [EE(nc.sbuf_tensor(f"acc_slots{i}", [128, NSG * D], BF16))
                     for i in range(2)]
        xg = [EE(nc.sbuf_tensor(f"xg{i}", [128, D + 4], F32)) for i in range(2)]
        selS = EE(nc.sbuf_tensor("selS", [128, SUB * E], F32))
        sel1_sb = EE(nc.sbuf_tensor("sel1_sb", [128, SUB * E], F32))
        w12_sb = EE(nc.sbuf_tensor("w12_sb", [128, SUB * 2], F32))
        posg = EE(nc.sbuf_tensor("posg", [128, SUB * E], F32))
        pr_sb = EE(nc.sbuf_tensor("pr_sb", [128, SUB * 2], F32))
        pr_i = EE(nc.sbuf_tensor("pr_i", [128, SUB * 2], mybir.dt.int32))
        selS_bf = EE(nc.sbuf_tensor("selS_bf", [128, SUB * E], BF16))
        cnt_sb = EE(nc.sbuf_tensor("cnt_sb", [128, SUB * E], F32))
        L_bf = EE(nc.sbuf_tensor("L_bf", [128, 128], BF16))
        onescol_bf = EE(nc.sbuf_tensor("onescol_bf", [128, 1], BF16))
        ones1_f = EE(nc.sbuf_tensor("ones1_f", [1, 128], F32))
        tot_row = EE(nc.sbuf_tensor("tot_row", [1, SUB * E], F32))
        offsA = EE(nc.sbuf_tensor("offsA", [1, SUB * E], F32))
        offsB = EE(nc.sbuf_tensor("offsB", [1, SUB * E], F32))
        base_row = EE(nc.sbuf_tensor("base_row", [1, SUB * E], F32))
        tki_f = EE(nc.sbuf_tensor("tki_f", [128, SUB], F32))
        tok_all = EE(nc.sbuf_tensor("tok_all", [128, 2 * NSG], mybir.dt.int32))
        rg_all = EE(nc.sbuf_tensor("rg_all", [128, 2 * NSG], F32))
        WB = DB * FCH + FB * D
        wbuf = [EE(nc.sbuf_tensor(f"wbuf{i}", [128, max(WB, 2 * DB * D)], BF16)) for i in range(2)]
        scr = EE(nc.sbuf_tensor("scr", [128, DB * D], BF16))
        qT_off = 0
        aoT_off = D
        xb = [EE(nc.sbuf_tensor(f"xb{i}", [128, D], F32)) for i in range(2)]
        xm = [EE(nc.sbuf_tensor(f"xm{i}", [128, D], F32)) for i in range(2)]
        xnf = [EE(nc.sbuf_tensor(f"xnf{i}", [128, D], F32)) for i in range(2)]
        xnb = [EE(nc.sbuf_tensor(f"xnb{i}", [128, D], BF16)) for i in range(2)]
        g_sb0 = EE(nc.sbuf_tensor("g_sb0", [128, D], F32))
        gb = EE(nc.sbuf_tensor("gb", [128, 2 * D], BF16))
        st_sb = [EE(nc.sbuf_tensor(f"st{i}", [128, 6 * NBN], F32)) for i in range(4)]
        ag_sb = [EE(nc.sbuf_tensor(f"ag{i}", [128, 2], F32)) for i in range(4)]
        rstd = [EE(nc.sbuf_tensor(f"rstd{i}", [128, 1], F32)) for i in range(4)]
        nmu = [EE(nc.sbuf_tensor(f"nmu{i}", [128, 1], F32)) for i in range(4)]
        nmur = [EE(nc.sbuf_tensor(f"nmur{i}", [128, 1], F32)) for i in range(4)]
        stu = EE(nc.sbuf_tensor("stu", [128, SUB * 6 * NBN], F32))
        agu = EE(nc.sbuf_tensor("agu", [128, SUB * 2], F32))
        rstdu = EE(nc.sbuf_tensor("rstdu", [128, SUB], F32))
        nmuu = EE(nc.sbuf_tensor("nmuu", [128, SUB], F32))
        nmuru = EE(nc.sbuf_tensor("nmuru", [128, SUB], F32))
        R_sb = EE(nc.sbuf_tensor("R_sb", [128, SUB * E], F32))
        RT4 = [EE(nc.sbuf_tensor(f"RT4_{i}", [E, 512], BF16)) for i in range(2)]
        lg_all = EE(nc.sbuf_tensor("lg_all", [128, SUB * E], F32))
        iota_big = EE(nc.sbuf_tensor("iota_big", [128, SUB * E], F32))
        mxa = EE(nc.sbuf_tensor("mxa", [128, 4 * SUB], F32))
        wc = EE(nc.sbuf_tensor("wc", [128, 2 * SUB], F32))
        bias_sb = EE(nc.sbuf_tensor("bias_sb", [128, D], BF16))
        eps_col = EE(nc.sbuf_tensor("eps_col", [128, 1], F32))
        id_f32 = EE(nc.sbuf_tensor("id_f32", [128, 128], F32))
        ones_bf = EE(nc.sbuf_tensor("ones_bf", [1, 128], BF16))
        wrt_bf = EE(nc.sbuf_tensor("wrt_bf", [128, DB * E], BF16))
        br_bf = EE(nc.sbuf_tensor("br_bf", [1, E], BF16))
        b1r_sb = EE(nc.sbuf_tensor("b1r_sb", [128, E * FPB], F32))
        b2_bf = EE(nc.sbuf_tensor("b2_bf", [E, D], BF16))
        B_bf0 = EE(nc.sbuf_tensor("B_bf0", [128, DB * DH], BF16))
        B_bf = [B_bf0, B_bf0]
        kv_bf = EE(nc.sbuf_tensor("kv_bf", [128, (H // 2) * DH], BF16))
        pb = [EE(nc.psum_tensor(f"pb{i}", [128, 512], F32)) for i in range(8)]

        sT = Sem(EE(nc.semaphore("sT")))
        sV = Sem(EE(nc.semaphore("sV")))
        sS = Sem(EE(nc.semaphore("sS")))
        sP = Sem(EE(nc.semaphore("sP")))
        sSmS = Sem(EE(nc.semaphore("sSmS")))
        sSmG = Sem(EE(nc.semaphore("sSmG")))
        sxb = [Sem(EE(nc.semaphore(f"sxb{i}"))) for i in range(2)]
        sxn = [Sem(EE(nc.semaphore(f"sxn{i}"))) for i in range(2)]
        sStr = [Sem(EE(nc.semaphore(f"sStr{i}"))) for i in range(2)]
        swb = [Sem(EE(nc.semaphore(f"swb{i}"))) for i in range(2)]
        sout = [Sem(EE(nc.semaphore(f"sout{i}"))) for i in range(2)]
        sDg = Sem(EE(nc.semaphore("sDg")))
        sxo2 = [Sem(EE(nc.semaphore(f"sxo{i}"))) for i in range(2)]
        sxT = [Sem(EE(nc.semaphore(f"sxT{i}"))) for i in range(2)]
        sRd = Sem(EE(nc.semaphore("sRd")))
        sac = [Sem(EE(nc.semaphore(f"sac{i}"))) for i in range(2)]
        szf = Sem(EE(nc.semaphore("szf")))
        ssc2 = [Sem(EE(nc.semaphore(f"ssc{i}"))) for i in range(2)]
        ssg = [Sem(EE(nc.semaphore(f"ssg{i}"))) for i in range(2)]
        sxg = [Sem(EE(nc.semaphore(f"sxg{i}"))) for i in range(2)]
        sst = [Sem(EE(nc.semaphore(f"sst{i}"))) for i in range(2)]
        szb = Sem(EE(nc.semaphore("szb")))
        sza = Sem(EE(nc.semaphore("sza")))
        stk = Sem(EE(nc.semaphore("stk")))
        srg = Sem(EE(nc.semaphore("srg")))
        block = EE(nc.Block())

        xnTs3 = [t_[:, :].rearrange("p (b t2) -> p b t2", b=DB) for t_ in xnTs]
        xgT3 = xgT[:, :].rearrange("p (b t) -> p b t", b=DB)

        def op(eng, fn, waits=(), inc=None, inck=1):
            prog[eng].append((fn, tuple(waits), inc, inck))
            if inc is not None:
                inc.n += inck
            return inc.n if inc is not None else None

        def vo(fn, waits=()):
            return op("vector", fn, waits=[(sV, sV.n)] + list(waits), inc=sV)

        def so(fn, waits=()):
            return op("scalar", fn, waits=[(sS, sS.n)] + list(waits), inc=sS)

        def to(fn, waits=(), last=True):
            return op("tensor", fn, waits=waits, inc=(sT if last else None))

        op("gpsimd", lambda g: g.memset(id_f32[:, :], 0.0), inc=sP)
        op("gpsimd", lambda g: g.affine_select(
            out=id_f32[:, :], in_=id_f32[:, :], compare_op=ALU.not_equal,
            fill=1.0, base=0, pattern=[[-1, 128]], channel_multiplier=1),
            waits=[(sP, sP.n)], inc=sP)
        op("gpsimd", lambda g: g.memset(ones_bf[:, :], 1.0), inc=sP)
        op("gpsimd", lambda g: g.iota(
            iota_big[:, :], pattern=[[0, SUB], [-1, E]], base=E, channel_multiplier=0,
            allow_small_or_imprecise_dtypes=True), inc=sP)
        op("gpsimd", lambda g: g.memset(eps_col[:, :], LN_EPS), inc=sP)
        op("gpsimd", lambda g: g.memset(L_bf[:, :], 0.0), inc=sP)
        op("gpsimd", lambda g: g.affine_select(
            out=L_bf[:, :], in_=L_bf[:, :], compare_op=ALU.is_gt,
            fill=1.0, base=0, pattern=[[-1, 128]], channel_multiplier=1),
            waits=[(sP, sP.n)], inc=sP)
        op("gpsimd", lambda g: g.memset(onescol_bf[:, :], 1.0), inc=sP)
        op("gpsimd", lambda g: g.memset(ones1_f[:, :], 1.0), inc=sP)
        op("gpsimd", lambda g: g.iota(
            tki_f[:, :], pattern=[[128, SUB]], base=0, channel_multiplier=1,
            allow_small_or_imprecise_dtypes=True), inc=sP)
        op("gpsimd", lambda g: g.iota(
            base_row[:, :].rearrange("o (s e) -> o s e", e=E),
            pattern=[[0, SUB], [CAP, E]], base=0, channel_multiplier=0,
            allow_small_or_imprecise_dtypes=True), inc=sP)
        op("gpsimd", lambda g: g.memset(xg[0][:, :], 0.0), inc=sP)
        op("gpsimd", lambda g: g.memset(xg[0][:, D:D + 1], float(TC)),
           waits=[(sP, sP.n)], inc=sP)
        op("gpsimd", lambda g: g.memset(rg_all[:, :], 0.0), inc=sP)
        p0 = sP.n
        for i in range(E * CAP // 128):
            op("sync", lambda s, i=i: s.dma_start(
                out=stage_x[i * 128:(i + 1) * 128, :], in_=xg[0][:, :]),
                waits=[(sP, p0)] if i == 0 else [], inc=szf, inck=16)
        zf_all = szf.n

        for b in range(DB):
            op("gpsimd", lambda g, b=b: g.dma_start(
                out=wrt_bf[:, b * E:(b + 1) * E],
                in_=wrt_d[b * 128:(b + 1) * 128, :]), inc=sSmG, inck=16)
        op("gpsimd", lambda g: g.dma_start(out=br_bf[:, :], in_=br_d), inc=sSmG, inck=16)
        op("gpsimd", lambda g: g.dma_start(out=b2_bf[:, :], in_=b2_d), inc=sSmG, inck=16)
        op("gpsimd", lambda g: g.dma_start(out=scr[0:1, 0:D], in_=gam_d), inc=sSmG, inck=16)
        op("gpsimd", lambda g: g.dma_start(out=scr[0:1, D:2 * D], in_=bet_d), inc=sSmG, inck=16)
        g_small_sem, g_small = sSmG, sSmG.n
        for b in range(DB):
            op("gpsimd", lambda g, b=b: g.dma_start(
                out=wbuf[1][:, b * D:(b + 1) * D],
                in_=wk_d[b * 128:(b + 1) * 128, :]), inc=swb[1], inck=16)
            op("gpsimd", lambda g, b=b: g.dma_start(
                out=wbuf[1][:, DB * D + b * D:DB * D + (b + 1) * D],
                in_=wv_d[b * 128:(b + 1) * 128, :]), inc=swb[1], inck=16)
        g_kvw = swb[1].n

        def chunk_load(ch, wait_t):
            e, q = ch // (F // FCH), ch % (F // FCH)
            par = ch % 2
            for b in range(DB):
                op("gpsimd", lambda g, e=e, q=q, par=par, b=b: g.dma_start(
                    out=wbuf[par][:, b * FCH:(b + 1) * FCH],
                    in_=w1t_d[e][b * 128:(b + 1) * 128, q * FCH:(q + 1) * FCH]),
                    waits=([(sT, wait_t)] if b == 0 else []), inc=swb[par], inck=16)
            for fbb in range(FB):
                op("gpsimd", lambda g, e=e, q=q, par=par, fbb=fbb: g.dma_start(
                    out=wbuf[par][:, DB * FCH + fbb * D:DB * FCH + (fbb + 1) * D],
                    in_=w2t_d[e][q * FCH + fbb * 128:q * FCH + (fbb + 1) * 128, :]),
                    inc=swb[par], inck=16)
            return swb[par].n

        ch_ready = {0: chunk_load(0, 0)}

        op("gpsimd", lambda e: e.dma_start(
            out=b1r_sb[:, :].rearrange("p (e f) -> p e f", e=E),
            in_=b1r_d.rearrange("e p f -> p e f")), inc=sSmS, inck=16)

        for jc in range(NJC):
            to(lambda t, jc=jc: t.matmul(pb[2 + jc][:, :], ones_bf[:, :],
                                         scr[0:1, jc * 512:(jc + 1) * 512],
                                         start=True, stop=True),
               waits=[(sP, p0), (sSmG, g_small)])
            to(lambda t, jc=jc: t.matmul(pb[2 + NJC + jc][:, :], ones_bf[:, :],
                                         scr[0:1, D + jc * 512:D + (jc + 1) * 512],
                                         start=True, stop=True))
        t_gb = sT.n
        for jc in range(NJC):
            vo(lambda v, jc=jc: v.tensor_copy(gb[:, jc * 512:(jc + 1) * 512], pb[2 + jc][:, :]),
               [(sT, t_gb)])
            vo(lambda v, jc=jc: v.tensor_copy(gb[:, D + jc * 512:D + (jc + 1) * 512],
                                              pb[2 + NJC + jc][:, :]))
        v_gb = sV.n

        ag_ln = [0, 0, 0, 0]
        nmu_guard = [0, 0, 0, 0]

        def ln_stats(par, src, waits_first):
            for c in range(NBN):
                vo(lambda v, c=c, par=par: v.bn_stats(st_sb[par][:, c * 6:(c + 1) * 6],
                                                      src[:, c * 512:(c + 1) * 512]),
                   waits_first if c == 0 else [])
            vo(lambda v, par=par: v.bn_aggr(ag_sb[par][:, :], st_sb[par][:, :]),
               [(sS, ag_ln[par])])
            vo(lambda v, par=par: v.tensor_scalar(nmu[par][:, :], ag_sb[par][:, 0:1],
                                                  -1.0, None, op0=ALU.mult),
               [(sS, nmu_guard[par])])
            va = sV.n
            so(lambda s, par=par: s.activation(rstd[par][:, :], ag_sb[par][:, 1:2],
                                               AF.Ln, bias=eps_col[:, 0:1]), [(sV, va), (sP, p0)])
            ag_ln[par] = sS.n
            so(lambda s, par=par: s.activation(rstd[par][:, :], rstd[par][:, :],
                                               AF.Exp, scale=-0.5))
            so(lambda s, par=par: s.activation(nmur[par][:, :], rstd[par][:, :],
                                               AF.Copy, scale=nmu[par][:, 0:1]))
            nmu_guard[par] = sS.n
            return sS.n

        xb_free = [0, 0]
        t_lgdone = [0, 0]
        xT_store = [0, 0]
        xnf_free_t = [0, 0]
        ac_cnt = [0, 0]
        xo_cnt = [0, 0]
        s_evac_prev = [0, 0]
        v_lgfree = 0
        for s_i in range(SUB):
            par = s_i % 2
            op("sync", lambda e, s_i=s_i, par=par: e.dma_start(
                out=xb[par][:, :], in_=x_shard[s_i * 128:(s_i + 1) * 128, :]),
                waits=[(sS, xb_free[par])], inc=sxb[par], inck=16)
            lx = sxb[par].n
            s_rdy = ln_stats(par, xb[par], [(sxb[par], lx)])
            so(lambda s, par=par: s.activation(
                xnf[par][:, :], xb[par][:, :], AF.Identity,
                scale=rstd[par][:, 0:1], bias=nmur[par][:, 0:1]),
               [(sT, xnf_free_t[par]), (sxo2[par], xo_cnt[par]),
                (sac[par], ac_cnt[par])])
            xb_free[par] = sS.n
            s_xn = sS.n
            op("sync", lambda e, s_i=s_i, par=par: e.dma_start(
                out=xn_own[s_i * 128:(s_i + 1) * 128, :], in_=xnf[par][:, :]),
                waits=[(sS, s_xn)], inc=sxo2[par], inck=16)
            xo_cnt[par] = sxo2[par].n
            op("sync", lambda e, s_i=s_i, par=par: e.dma_start(
                out=acc_dram[s_i * 128:(s_i + 1) * 128, :], in_=xnf[par][:, :]),
                waits=[(sS, s_xn)], inc=sac[par], inck=16)
            ac_cnt[par] = sac[par].n
            tb = 0 if par == 0 else 4
            for b in range(DB):
                to(lambda t, b=b, par=par, tb=tb: t.transpose(
                    pb[tb + b // 4][:, (b % 4) * 128:(b % 4 + 1) * 128],
                    xnf[par][:, b * 128:(b + 1) * 128], id_f32[:, :]),
                   waits=([(sS, s_xn), (sS, s_evac_prev[par])] +
                          ([(sP, p0)] if s_i == 0 else []) +
                          ([(sV, v_gb)] if s_i == 1 else [])) if b == 0 else [],
                   last=(b == DB - 1))
            xnf_free_t[par] = sT.n
            t_tr = sT.n
            nb0 = min(4, DB)
            so(lambda s, par=par, nb0=nb0, tb=tb: s.copy(
                xnTs3[par][:, 0:nb0, 0:128],
                pb[tb][:, 0:nb0 * 128].rearrange("p (b t) -> p b t", b=nb0)),
               [(sT, t_tr), (sT, t_lgdone[par]), (sxT[par], xT_store[par])])
            if DB > 4:
                so(lambda s, par=par, tb=tb: s.copy(
                    xnTs3[par][:, 4:DB, 0:128],
                    pb[tb + 1][:, 0:(DB - 4) * 128].rearrange("p (b t) -> p b t", b=DB - 4)))
            s_evac_prev[par] = sS.n
            s_xnT = sS.n
            op("sync", lambda e, s_i=s_i, par=par: e.dma_start(
                out=xnT_dram[:, s_i * DB * 128:(s_i + 1) * DB * 128],
                in_=xnTs[par][:, :]),
                waits=[(sS, s_xnT)], inc=sxT[par], inck=16)
            xT_store[par] = sxT[par].n
            for b in range(DB):
                op("tensor", lambda t, b=b, par=par: t.matmul(
                    pb[2][:, 0:E], xnTs3[par][:, b, 0:128],
                    wrt_bf[:, b * E:(b + 1) * E], start=(b == 0), stop=False),
                    waits=[(sS, s_xnT), (sSmG, g_small), (sV, v_gb if s_i == 0 else v_lgfree)]
                    if b == 0 else [])
            to(lambda t: t.matmul(pb[2][:, 0:E], ones_bf[:, :], br_bf[:, :],
                                  start=False, stop=True))
            t_lg = sT.n
            t_lgdone[par] = sT.n
            vo(lambda v, s_i=s_i: v.tensor_copy(lg_all[:, s_i * E:(s_i + 1) * E],
                                                pb[2][:, 0:E]), [(sT, t_lg)])
            v_lgfree = sV.n

        rt2a, rt3a = posg, cnt_sb
        lg3 = lg_all[:, :].rearrange("p (s e) -> p s e", e=E)
        rt2a3 = rt2a[:, :].rearrange("p (s e) -> p s e", e=E)
        rt3a3 = rt3a[:, :].rearrange("p (s e) -> p s e", e=E)

        def mxv(i):
            return mxa[:, i * SUB:(i + 1) * SUB].rearrange("p (s o) -> p s o", o=1)
        vo(lambda v: v.reduce_max(mxv(0)[:, :, :], lg3, axis=AX))
        vo(lambda v: v.tensor_tensor(rt2a3, lg3, mxv(0).broadcast_to((128, SUB, E)),
                                     op=ALU.is_ge), [(sP, p0)])
        vo(lambda v: v.tensor_mul(rt2a[:, :], rt2a[:, :], iota_big[:, :]))
        vo(lambda v: v.reduce_max(mxv(1)[:, :, :], rt2a3, axis=AX))
        vo(lambda v: v.tensor_tensor(rt2a3, rt2a3, mxv(1).broadcast_to((128, SUB, E)),
                                     op=ALU.is_ge))
        vo(lambda v: v.tensor_copy(sel1_sb[:, :], rt2a[:, :]))
        vo(lambda v: v.scalar_tensor_tensor(rt3a[:, :], rt2a[:, :], -1e30, lg_all[:, :],
                                            op0=ALU.mult, op1=ALU.add))
        vo(lambda v: v.reduce_max(mxv(2)[:, :, :], rt3a3, axis=AX))
        vo(lambda v: v.tensor_tensor(rt3a3, rt3a3, mxv(2).broadcast_to((128, SUB, E)),
                                     op=ALU.is_ge))
        vo(lambda v: v.tensor_mul(rt3a[:, :], rt3a[:, :], iota_big[:, :]))
        vo(lambda v: v.reduce_max(mxv(3)[:, :, :], rt3a3, axis=AX))
        vo(lambda v: v.tensor_tensor(rt3a3, rt3a3, mxv(3).broadcast_to((128, SUB, E)),
                                     op=ALU.is_ge))
        vo(lambda v: v.tensor_add(selS[:, :], rt2a[:, :], rt3a[:, :]))
        vo(lambda v: v.tensor_sub(wc[:, 0:SUB], mxa[:, 2 * SUB:3 * SUB], mxa[:, 0:SUB]))
        v_d = sV.n
        so(lambda s: s.activation(wc[:, SUB:2 * SUB], wc[:, 0:SUB], AF.Exp), [(sV, v_d)])
        s_e = sS.n
        vo(lambda v: v.tensor_scalar(wc[:, SUB:2 * SUB], wc[:, SUB:2 * SUB], 1.0, None,
                                     op0=ALU.add), [(sS, s_e)])
        vo(lambda v: v.reciprocal(wc[:, 0:SUB], wc[:, SUB:2 * SUB]))
        vo(lambda v: v.tensor_scalar(wc[:, SUB:2 * SUB], wc[:, 0:SUB], -1.0, -1.0,
                                     op0=ALU.mult, op1=ALU.subtract))
        w12v = w12_sb[:, :].rearrange("p (s r) -> p s r", r=2)
        w1v = wc[:, 0:SUB].rearrange("p (s o) -> p s o", o=1)
        w2v = wc[:, SUB:2 * SUB].rearrange("p (s o) -> p s o", o=1)
        vo(lambda v: v.tensor_copy(w12v[:, :, 0:1], w1v[:, :, :]))
        vo(lambda v: v.tensor_copy(w12v[:, :, 1:2], w2v[:, :, :]))
        vo(lambda v: v.tensor_tensor(rt2a3, rt2a3, w1v.broadcast_to((128, SUB, E)),
                                     op=ALU.mult))
        vo(lambda v: v.tensor_tensor(rt3a3, rt3a3, w2v.broadcast_to((128, SUB, E)),
                                     op=ALU.mult))
        vo(lambda v: v.tensor_add(R_sb[:, :], rt2a[:, :], rt3a[:, :]))
        v_R = sV.n

        s_rt4_used = [0, 0]
        t_rt4_read = [0, 0]
        s_bias_prev = 0
        szb_prev = 0
        for r in range((SUB + 3) // 4):
            rp = r % 2
            ks = list(range(4 * r, min(4 * r + 4, SUB)))
            for j, s_i in enumerate(ks):
                to(lambda t, s_i=s_i, j=j: t.transpose(
                    pb[6][0:E, j * 128:(j + 1) * 128],
                    R_sb[:, s_i * E:(s_i + 1) * E], id_f32[:, :]),
                   waits=([(sV, v_R), (sS, s_rt4_used[rp])] if j == 0 else []),
                   last=(j == len(ks) - 1))
            t_rt = sT.n
            so(lambda s, rp=rp, n=len(ks): s.copy(RT4[rp][0:E, 0:n * 128],
                                                  pb[6][0:E, 0:n * 128]),
               [(sT, t_rt), (sT, t_rt4_read[rp])])
            s_rt4 = sS.n
            s_rt4_used[rp] = sS.n
            for j, s_i in enumerate(ks):
                for jc in range(NJC):
                    to(lambda t, j=j, jc=jc, rp=rp: t.matmul(
                        pb[2 + jc][:, :], RT4[rp][0:E, j * 128:(j + 1) * 128],
                        b2_bf[0:E, jc * 512:(jc + 1) * 512],
                        start=True, stop=True),
                       waits=(([(sS, s_rt4), (sSmG, g_small)] if j == 0 else []) +
                              [(sS, s_bias_prev)]) if jc == 0 else [])
                t_bias = sT.n
                for jc in range(NJC):
                    so(lambda s, jc=jc: s.copy(bias_sb[:, jc * 512:(jc + 1) * 512],
                                               pb[2 + jc][:, :]),
                       [(sT, t_bias)] + ([(szb, szb_prev)] if jc == 0 else []))
                s_bias_prev = sS.n
                op("gpsimd", lambda g, s_i=s_i: g.dma_start(
                    out=acc_dram[s_i * 128:(s_i + 1) * 128, :], in_=bias_sb[:, :],
                    accum_op=ALU.add),
                    waits=[(sS, s_bias_prev), (sac[s_i % 2], ac_cnt[s_i % 2])],
                    inc=szb, inck=16)
                szb_prev = szb.n
            t_rt4_read[rp] = sT.n
        s_bias_done = sS.n
        szb_all = szb.n

        xnb_free = [0, 0]
        for t_i in range(TILES):
            par = t_i % 2
            op("sync", lambda e, t_i=t_i, par=par: e.dma_start(
                out=xb[par][:, :], in_=x_full[t_i * 128:(t_i + 1) * 128, :]),
                waits=[(sS, xb_free[par])], inc=sxb[par], inck=16)
            lx = sxb[par].n
            s_rdy = ln_stats(par, xb[par], [(sxb[par], lx)])
            so(lambda s, par=par: s.activation(
                xnb[par][:, :], xb[par][:, :], AF.Identity,
                scale=rstd[par][:, 0:1], bias=nmur[par][:, 0:1]),
               [(sT, xnb_free[par]), (sStr[par], 16 * (t_i // 2))])
            xb_free[par] = sS.n
            s_xn = sS.n
            op("gpsimd", lambda g, t_i=t_i, par=par: g.dma_start(
                out=xn_scr[t_i * 128:(t_i + 1) * 128, :], in_=xnb[par][:, :]),
                waits=[(sS, s_xn)], inc=sStr[par], inck=16)
            for ib in range(DB):
                w = []
                if ib == 0:
                    w = [(sS, s_xn)]
                    if t_i == 0:
                        w.append((sS, s_bias_done))
                to(lambda t, ib=ib, par=par, t_i=t_i: t.matmul(
                    pb[ib][:, 0:PAN], xnb[par][:, ib * 128:(ib + 1) * 128],
                    xnb[par][:, 0:PAN], start=(t_i == 0), stop=(t_i == TILES - 1)),
                   waits=w, last=(ib == DB - 1))
            xnb_free[par] = sT.n
        t_pan0 = sT.n
        for ib in range(DB):
            so(lambda s, ib=ib: s.copy(scr[:, ib * D:ib * D + PAN], pb[ib][:, 0:PAN]),
               [(sT, t_pan0)] if ib == 0 else [])
        HB = DB // 2
        for c in range(HB):
            so(lambda s, c=c: s.copy(
                xm[c // 2][:, (c % 2) * PAN:(c % 2 + 1) * PAN],
                pb[HB + c][:, 0:PAN]))
        s_pan0 = sS.n

        vo(lambda v: v.tensor_copy(selS_bf[:, :], selS[:, :]))
        v_selS = sV.n
        for s_i in range(SUB):
            to(lambda t, s_i=s_i: t.matmul(
                pb[4][:, s_i * E:(s_i + 1) * E], L_bf[:, :],
                selS_bf[:, s_i * E:(s_i + 1) * E], start=True, stop=True),
               waits=[(sV, v_selS), (sP, p0), (sS, s_pan0)] if s_i == 0 else [])
        to(lambda t: t.matmul(pb[5][0:1, 0:SUB * E], onescol_bf[:, :],
                              selS_bf[:, :], start=True, stop=True))
        t_cnt = sT.n
        vo(lambda v: v.tensor_copy(cnt_sb[:, :], pb[4][:, 0:SUB * E]), [(sT, t_cnt)])
        vo(lambda v: v.tensor_copy(tot_row[:, :], pb[5][0:1, 0:SUB * E]))
        vo(lambda v: v.memset(offsA[:, :], 0.0))
        vo(lambda v: v.tensor_copy(offsA[:, E:SUB * E], tot_row[:, 0:(SUB - 1) * E]))
        cur, oth = offsA, offsB
        sh = E
        while sh < SUB * E:
            vo(lambda v, cur=cur, oth=oth, sh=sh: v.tensor_copy(oth[:, 0:sh], cur[:, 0:sh]))
            vo(lambda v, cur=cur, oth=oth, sh=sh: v.tensor_add(
                oth[:, sh:SUB * E], cur[:, sh:SUB * E], cur[:, 0:SUB * E - sh]))
            cur, oth = oth, cur
            sh *= 2
        vo(lambda v, cur=cur: v.tensor_add(cur[:, :], cur[:, :], base_row[:, :]),
           [(sP, p0)])
        v_offs = sV.n
        to(lambda t, cur=cur: t.matmul(pb[5][:, 0:SUB * E], ones1_f[:, :], cur[:, :],
                                       start=True, stop=True),
           waits=[(sV, v_offs)])
        t_offs = sT.n
        vo(lambda v: v.tensor_scalar(cnt_sb[:, :], cnt_sb[:, :], -1.0, None,
                                     op0=ALU.add))
        vo(lambda v: v.tensor_add(posg[:, :], cnt_sb[:, :], pb[5][:, 0:SUB * E]),
           [(sT, t_offs)])
        pr3 = pr_sb[:, :].rearrange("p (s r) -> p s r", r=2)
        vo(lambda v: v.tensor_mul(selS[:, :], selS[:, :], posg[:, :]))
        vo(lambda v: v.tensor_mul(sel1_sb[:, :], sel1_sb[:, :], posg[:, :]))
        s13 = sel1_sb[:, :].rearrange("p (s e) -> p s e", e=E)
        sS3 = selS[:, :].rearrange("p (s e) -> p s e", e=E)
        vo(lambda v: v.reduce_sum(pr3[:, :, 0:1], s13, axis=AX))
        vo(lambda v: v.reduce_sum(pr3[:, :, 1:2], sS3, axis=AX))
        vo(lambda v: v.tensor_sub(pr_sb[:, :].rearrange("p (s r) -> p s r", r=2)[:, :, 1],
                                  pr_sb[:, :].rearrange("p (s r) -> p s r", r=2)[:, :, 1],
                                  pr_sb[:, :].rearrange("p (s r) -> p s r", r=2)[:, :, 0]))
        vo(lambda v: v.tensor_copy(pr_i[:, :], pr_sb[:, :]))
        v_posi = sV.n

        sc_guard = [0, 0]
        for s_i in range(SUB):
            for rk in range(2):
                p2 = (2 * s_i + rk) % 2
                op("gpsimd", lambda e, s_i=s_i, p2=p2: e.dma_start(
                    out=xg[p2][:, 0:D], in_=xn_own[s_i * 128:(s_i + 1) * 128, :]),
                    waits=[(szf, zf_all), (ssc2[p2], sc_guard[p2]), (sV, v_posi)] +
                          ([(sxo2[0], xo_cnt[0]), (sxo2[1], xo_cnt[1])]
                           if s_i == 0 else []),
                    inc=ssg[p2], inck=16)
                lxs = ssg[p2].n
                vo(lambda v, s_i=s_i, p2=p2: v.tensor_copy(
                    xg[p2][:, D:D + 1], tki_f[:, s_i:s_i + 1]), [(ssg[p2], lxs)])
                vo(lambda v, s_i=s_i, rk=rk, p2=p2: v.tensor_copy(
                    xg[p2][:, D + 1:D + 2], w12_sb[:, 2 * s_i + rk:2 * s_i + rk + 1]))
                v_xgw = sV.n
                op("gpsimd", lambda g, s_i=s_i, rk=rk, p2=p2: g.indirect_dma_start(
                    out=stage_x,
                    out_offset=bass.IndirectOffsetOnAxis(
                        pr_i[:, 2 * s_i + rk:2 * s_i + rk + 1], 0),
                    in_=xg[p2][:, 0:D + 2], in_offset=None,
                    bounds_check=None, oob_is_err=False),
                    waits=[(sV, v_xgw), (sP, p0)],
                    inc=ssc2[p2], inck=16)
                sc_guard[p2] = ssc2[p2].n
        sc_all = [ssc2[0].n, ssc2[1].n]

        ctr_evac = [s_pan0, s_pan0]
        for a in range(HB):
            for c in range(HB):
                to(lambda t, a=a, c=c: t.transpose(
                    pb[a % 2][:, c * 128:(c + 1) * 128],
                    xm[c // 2][:, (c % 2) * PAN + a * 128:(c % 2) * PAN + (a + 1) * 128],
                    id_f32[:, :]),
                   waits=([(sS, ctr_evac[a % 2])] if c == 0 else []),
                   last=(c == HB - 1))
            t_ctr = sT.n
            so(lambda s, a=a: s.copy(
                scr[:, a * D + PAN:(a + 1) * D],
                pb[a % 2][:, 0:HB * 128]), [(sT, t_ctr)])
            ctr_evac[a % 2] = sS.n
        s_ctr = sS.n
        xnb_ld_free = [t_pan0, t_pan0]
        for t_i in range(TILES):
            par = t_i % 2
            op("sync", lambda e, t_i=t_i, par=par: e.dma_start(
                out=xnb[par][:, PAN:D], in_=xn_scr[t_i * 128:(t_i + 1) * 128, PAN:D]),
                waits=[(sStr[0], sStr[0].n), (sStr[1], sStr[1].n),
                       (sT, xnb_ld_free[par])], inc=sxn[par], inck=16)
            lx = sxn[par].n
            for ib in range(HB, DB):
                w = []
                if ib == HB:
                    w = [(sxn[par], lx)]
                    if t_i == 0:
                        w.append((sS, s_pan0))
                        w.append((sV, v_posi))
                to(lambda t, ib=ib, par=par, t_i=t_i: t.matmul(
                    pb[ib][:, 0:PAN], xnb[par][:, ib * 128:(ib + 1) * 128],
                    xnb[par][:, PAN:D], start=(t_i == 0), stop=(t_i == TILES - 1)),
                   waits=w, last=(ib == DB - 1))
            xnb_ld_free[par] = sT.n
        t_pan1 = sT.n
        for ib in range(HB, DB):
            so(lambda s, ib=ib: s.copy(scr[:, ib * D + PAN:(ib + 1) * D], pb[ib][:, 0:PAN]),
               [(sT, t_pan1)] if ib == HB else [])
        s_C = sS.n

        s_hfree = {0: 0, 1: 0}
        v_ofree = [v_posi, v_posi]
        t_xgfree = [0, 0]
        v_xgfree = [0, 0]
        s_trevac = {(6, 7): 0, (2, 3): s_bias_done}
        za_done = [0, 0]
        s_xgT_exp = {}

        def emit_pro(e_i, glo, ghi, banks=(6, 7)):
            ep = e_i % 2
            for g_i in range(glo, ghi):
                xp = g_i % 2
                op("sync", lambda e, e_i=e_i, g_i=g_i, xp=xp: e.dma_start(
                    out=xg[xp][:, :],
                    in_=stage_x[e_i * CAP + g_i * 128:e_i * CAP + (g_i + 1) * 128, :]),
                    waits=[(sT, t_xgfree[xp]), (sV, v_xgfree[xp])] +
                          ([(ssc2[0], sc_all[0]), (ssc2[1], sc_all[1])]
                           if (e_i == 0 and g_i < 2) else []),
                    inc=sxg[xp], inck=16)
                lxg = sxg[xp].n
                vo(lambda v, g_i=g_i, xp=xp, ep=ep: v.tensor_copy(
                    tok_all[:, ep * NSG + g_i:ep * NSG + g_i + 1], xg[xp][:, D:D + 1]),
                   [(sxg[xp], lxg)] +
                   ([(sza, za_done[ep])] if g_i == 0 else []))
                vo(lambda v, g_i=g_i, xp=xp, ep=ep: v.tensor_copy(
                    rg_all[:, ep * NSG + g_i:ep * NSG + g_i + 1], xg[xp][:, D + 1:D + 2]))
                v_xgfree[xp] = sV.n
                for b in range(DB):
                    to(lambda t, b=b, xp=xp, banks=banks: t.transpose(
                        pb[banks[b // 4]][:, (b % 4) * 128:(b % 4 + 1) * 128],
                        xg[xp][:, b * 128:(b + 1) * 128], id_f32[:, :]),
                       waits=([(sxg[xp], lxg), (sS, s_trevac[banks])] if b == 0 else []),
                       last=(b == DB - 1))
                t_xgfree[xp] = sT.n
                t_tr = sT.n
                nb0 = min(4, DB)
                so(lambda s, g_i=g_i, nb0=nb0, banks=banks: s.copy(
                    xgT3[:, 0:nb0, g_i * 128:(g_i + 1) * 128],
                    pb[banks[0]][:, 0:nb0 * 128].rearrange("p (b t) -> p b t", b=nb0)),
                   [(sT, t_tr)])
                if DB > 4:
                    so(lambda s, g_i=g_i, banks=banks: s.copy(
                        xgT3[:, 4:DB, g_i * 128:(g_i + 1) * 128],
                        pb[banks[1]][:, 0:(DB - 4) * 128].rearrange("p (b t) -> p b t", b=DB - 4)))
                s_trevac[banks] = sS.n
            s_xgT_exp[e_i] = sS.n

        s_Bfree = [s_C, s_C]
        for h in range(H):
            par = h % 2
            for ib in range(DB):
                for jb in range(DB):
                    w = []
                    if ib == 0 and jb == 0:
                        w = [(sS, s_C), (swb[1], g_kvw), (sS, s_Bfree[par])]
                    to(lambda t, ib=ib, jb=jb, par=par, h=h: t.matmul(
                        pb[par][:, ib * DH:(ib + 1) * DH],
                        scr[:, jb * D + ib * 128:jb * D + (ib + 1) * 128],
                        wbuf[1][:, jb * D + h * DH:jb * D + (h + 1) * DH],
                        start=(jb == 0), stop=(jb == DB - 1)),
                       waits=w, last=(ib == DB - 1 and jb == DB - 1))
            t_B = sT.n
            so(lambda s, par=par: s.copy(B_bf[par][:, :], pb[par][:, 0:DB * DH]), [(sT, t_B)])
            s_Bfree[par] = sS.n
            sBh = sS.n
            for ib in range(DB):
                to(lambda t, ib=ib, par=par, h=h: t.matmul(
                    pb[6][(h % 2) * 64:(h % 2) * 64 + 64, (h // 2) * DH:(h // 2 + 1) * DH],
                    wbuf[1][:, DB * D + ib * D + h * DH:DB * D + ib * D + (h + 1) * DH],
                    B_bf[par][:, ib * DH:(ib + 1) * DH],
                    start=(ib == 0), stop=(ib == DB - 1)),
                   waits=[(sS, sBh)] if ib == 0 else [], last=(ib == DB - 1))
            if h >= H - NSG:
                emit_pro(0, h - (H - NSG), h - (H - NSG) + 1, banks=(2, 3))
        t_kv = sT.n
        so(lambda s: s.activation(kv_bf[:, :], pb[6][:, 0:(H // 2) * DH], AF.Copy,
                                  scale=SCALE), [(sT, t_kv)])
        s_kve = sS.n
        s_trevac[(6, 7)] = s_kve

        if dbg:
            op("sync", lambda e: e.dma_start(out=dbg_R, in_=R_sb[:, :]),
               waits=[(sV, v_R)], inc=sDg, inck=16)
            op("sync", lambda e: e.dma_start(out=dbg_C, in_=scr[:, 0:2048]),
               waits=[(sS, s_C)], inc=sDg, inck=16)
            op("sync", lambda e: e.dma_start(out=dbg_kv, in_=kv_bf[:, 0:512]),
               waits=[(sS, sS.n)], inc=sDg, inck=16)
            dbg_pre_ffn = sDg.n

        if NCH > 1:
            ch_ready[1] = chunk_load(1, t_kv)

        for ch in range(NCH):
            e_i, q_i = ch // NQ, ch % NQ
            par = ch % 2
            ep = e_i % 2
            vg_grp = {}
            goff = 0
            for gw in SGW:
                for fb in range(FB):
                    hb_ = fb % 2
                    for b in range(DB):
                        w = []
                        if b == 0:
                            w = [(swb[par], ch_ready[ch]), (sS, s_hfree[hb_])]
                            if q_i == 0 and goff == 0 and fb == 0:
                                w.append((sS, s_xgT_exp[e_i]))
                        to(lambda t, fb=fb, b=b, goff=goff, gw=gw, par=par, hb_=hb_: t.matmul(
                            pb[hb_][:, 0:gw],
                            wbuf[par][:, b * FCH + fb * 128:b * FCH + (fb + 1) * 128],
                            xgT3[:, b, goff:goff + gw],
                            start=(b == 0), stop=(b == DB - 1)),
                           waits=w, last=(b == DB - 1))
                    t_h = sT.n
                    so(lambda s, fb=fb, e_i=e_i, q_i=q_i, gw=gw, hb_=hb_: s.activation(
                        scr[:, fb * 512:fb * 512 + gw], pb[hb_][:, 0:gw], AF.Relu,
                        bias=b1r_sb[:, e_i * FPB + q_i * FB + fb:e_i * FPB + q_i * FB + fb + 1]),
                       [(sT, t_h)] + ([(sSmS, 16)] if (ch == 0 and goff == 0 and fb == 0) else []))
                    s_hfree[hb_] = sS.n
                s_hT_all = sS.n
                for sl in range(gw // 128):
                    sg_i = goff // 128 + sl
                    spar = sg_i % 2
                    for jc in range(NJC):
                        bank = 2 + spar * 2 + jc
                        for fb in range(FB):
                            w = []
                            if fb == 0:
                                w = [(sS, s_hT_all), (sV, v_ofree[spar])]
                            to(lambda t, fb=fb, sl=sl, jc=jc, par=par, bank=bank: t.matmul(
                                pb[bank][:, :],
                                scr[:, fb * 512 + sl * 128:fb * 512 + (sl + 1) * 128],
                                wbuf[par][:, DB * FCH + fb * D + jc * 512:DB * FCH + fb * D + (jc + 1) * 512],
                                start=(fb == 0), stop=(fb == FB - 1)),
                               waits=w, last=(fb == FB - 1))
                    t_o = sT.n
                    for jc in range(NJC):
                        bank = 2 + spar * 2 + jc
                        if q_i == 0:
                            vo(lambda v, sg_i=sg_i, jc=jc, bank=bank, ep=ep: v.tensor_scalar(
                                acc_slots[ep][:, sg_i * D + jc * 512:sg_i * D + (jc + 1) * 512],
                                pb[bank][:, :],
                                rg_all[:, ep * NSG + sg_i:ep * NSG + sg_i + 1], None,
                                op0=ALU.mult),
                               [(sT, t_o)] +
                               ([(sza, za_done[ep])]
                                if (goff == 0 and sl == 0 and jc == 0) else []))
                        else:
                            vo(lambda v, sg_i=sg_i, jc=jc, bank=bank, ep=ep: v.scalar_tensor_tensor(
                                acc_slots[ep][:, sg_i * D + jc * 512:sg_i * D + (jc + 1) * 512],
                                pb[bank][:, :],
                                rg_all[:, ep * NSG + sg_i:ep * NSG + sg_i + 1],
                                acc_slots[ep][:, sg_i * D + jc * 512:sg_i * D + (jc + 1) * 512],
                                op0=ALU.mult, op1=ALU.add),
                               [(sT, t_o)])
                    v_ofree[spar] = sV.n
                    if q_i == NQ - 1:
                        vg_grp[sg_i] = sV.n
                if q_i == NQ - 1 and e_i + 1 < E:
                    emit_pro(e_i + 1, goff // 128, (goff + gw) // 128)
                goff += gw
            if ch + 2 < NCH:
                ch_ready[ch + 2] = chunk_load(ch + 2, sT.n)
            if ch == NCH - 2:
                p3par = NCH % 2
                for b in range(DB):
                    op("gpsimd", lambda g, b=b, p3par=p3par: g.dma_start(
                        out=wbuf[p3par][:, b * D:(b + 1) * D],
                        in_=wqt_d[b * 128:(b + 1) * 128, :]),
                        waits=([(sT, sT.n)] if b == 0 else []), inc=swb[p3par], inck=16)
                    op("gpsimd", lambda g, b=b, p3par=p3par: g.dma_start(
                        out=wbuf[p3par][:, DB * D + b * D:DB * D + (b + 1) * D],
                        in_=wot_d[b * 128:(b + 1) * 128, :]), inc=swb[p3par], inck=16)
                g_p3w = swb[p3par].n
            if q_i == NQ - 1:
                for g_i in range(NSG):
                    w = [(sza, sza.n), (sV, vg_grp[g_i])]
                    if g_i == 0:
                        if e_i == 0:
                            w += [(sac[0], sac[0].n), (sac[1], sac[1].n),
                                  (szb, szb_all)]
                    op("gpsimd", lambda g, g_i=g_i, ep=ep: g.indirect_dma_start(
                        out=acc_dram,
                        out_offset=bass.IndirectOffsetOnAxis(
                            tok_all[:, ep * NSG + g_i:ep * NSG + g_i + 1], 0),
                        in_=acc_slots[ep][:, g_i * D:(g_i + 1) * D], in_offset=None,
                        bounds_check=None, oob_is_err=False,
                        compute_op=ALU.add),
                        waits=w, inc=sza, inck=16)
                za_done[ep] = sza.n
        t_ffn = sT.n
        v_ffn = sV.n

        NPAIR = H // 2
        ppw = max(1, 8 // NJC)
        first_N = True
        s_N_prev = 0
        pair = 0
        while pair < NPAIR:
            wave = list(range(pair, min(pair + ppw, NPAIR)))
            for i, pr in enumerate(wave):
                for h2 in range(2):
                    for jc in range(NJC):
                        w = []
                        if first_N:
                            w = [(sS, s_hfree[0]), (sS, s_hfree[1]),
                                 (sV, v_ofree[0]), (sV, v_ofree[1]),
                                 (sS, s_trevac[(6, 7)]), (sS, s_trevac[(2, 3)]),
                                 (sS, s_kve), (swb[p3par], g_p3w)]
                            first_N = False
                        elif i == 0 and h2 == 0 and jc == 0 and s_N_prev:
                            w = [(sS, s_N_prev)]
                        to(lambda t, i=i, pr=pr, h2=h2, jc=jc: t.matmul(
                            pb[i * NJC + jc][h2 * 64:h2 * 64 + 64, :],
                            kv_bf[h2 * 64:h2 * 64 + 64, pr * DH:(pr + 1) * DH],
                            wbuf[p3par][h2 * 64:h2 * 64 + 64,
                                        DB * D + pr * D + jc * 512:
                                        DB * D + pr * D + (jc + 1) * 512],
                            start=True, stop=True),
                           waits=w,
                           last=(i == len(wave) - 1 and h2 == 1 and jc == NJC - 1))
            t_N = sT.n
            for i, pr in enumerate(wave):
                for jc in range(NJC):
                    so(lambda s, i=i, pr=pr, jc=jc: s.copy(
                        scr[:, pr * D + jc * 512:pr * D + (jc + 1) * 512],
                        pb[i * NJC + jc][:, :]),
                       [(sT, t_N)] if (i == 0 and jc == 0) else [])
            s_N_prev = sS.n
            pair += ppw
        s_N_all = sS.n

        v_Gevac_prev = v_ffn
        NH2 = H // 2
        for jc2 in range(NJC):
            for half in range(2):
                dbs = list(range(half * (DB // 2), (half + 1) * (DB // 2)))
                for i, db in enumerate(dbs):
                    for h2 in range(2):
                        for k in range(NH2):
                            h = 2 * k + h2
                            bank = i + h2 * 4
                            w = []
                            if k == 0 and h2 == 0 and i == 0:
                                w = [(sS, s_N_all), (sV, v_Gevac_prev),
                                     (sV, v_ofree[0]), (sV, v_ofree[1])]
                            to(lambda t, db=db, h=h, jc2=jc2, bank=bank, k=k: t.matmul(
                                pb[bank][:, :],
                                wbuf[p3par][(h % 2) * 64:(h % 2) * 64 + 64,
                                            (h // 2) * D + db * 128:(h // 2) * D + (db + 1) * 128],
                                scr[(h % 2) * 64:(h % 2) * 64 + 64,
                                    (h // 2) * D + jc2 * 512:(h // 2) * D + (jc2 + 1) * 512],
                                start=(k == 0), stop=(k == NH2 - 1)),
                               waits=w, last=(k == NH2 - 1))
                t_G = sT.n
                for i, db in enumerate(dbs):
                    so(lambda s, db=db, i=i, jc2=jc2: s.copy(
                        G_sb[:, db * D + jc2 * 512:db * D + (jc2 + 1) * 512],
                        pb[i + 4][:, :]),
                       [(sT, t_G)] if i == 0 else [])
                s_Ghalf = sS.n
                for i, db in enumerate(dbs):
                    vo(lambda v, db=db, i=i, jc2=jc2: v.tensor_tensor(
                        G_sb[:, db * D + jc2 * 512:db * D + (jc2 + 1) * 512],
                        pb[i][:, :],
                        G_sb[:, db * D + jc2 * 512:db * D + (jc2 + 1) * 512],
                        op=ALU.add),
                       [(sS, s_Ghalf)] if i == 0 else [])
                v_Gevac_prev = sV.n
        v_G_all = sV.n

        agu2 = agu[:, :].rearrange("p (s two) -> p s two", two=2)
        ub_free = [sV.n, sV.n]
        for s_i in range(SUB):
            par = s_i % 2
            op("sync", lambda e, s_i=s_i, par=par: e.dma_start(
                out=xb[par][:, :], in_=acc_dram[s_i * 128:(s_i + 1) * 128, :]),
                waits=[(sV, ub_free[par])] + ([(sza, sza.n)] if s_i == 0 else []),
                inc=sxb[par], inck=16)
            lu = sxb[par].n
            for c in range(NBN):
                vo(lambda v, c=c, s_i=s_i, par=par: v.bn_stats(
                    stu[:, (s_i * NBN + c) * 6:(s_i * NBN + c) * 6 + 6],
                    xb[par][:, c * 512:(c + 1) * 512]),
                   [(sxb[par], lu)] if c == 0 else [])
            vo(lambda v, s_i=s_i: v.bn_aggr(agu[:, s_i * 2:(s_i + 1) * 2],
                                            stu[:, s_i * NBN * 6:(s_i + 1) * NBN * 6]))
            ub_free[par] = sV.n
        vo(lambda v: v.tensor_scalar(nmuu[:, :], agu2[:, :, 0], -1.0, None, op0=ALU.mult))
        v_stats = sV.n
        so(lambda s: s.activation(rstdu[:, :], agu2[:, :, 1], AF.Ln, bias=eps_col[:, 0:1]),
           [(sV, v_stats)])
        so(lambda s: s.activation(rstdu[:, :], rstdu[:, :], AF.Exp, scale=-0.5))
        s_rstdu = sS.n
        vo(lambda v: v.tensor_mul(nmuru[:, :], rstdu[:, :], nmuu[:, :]), [(sS, s_rstdu)])
        v_nmuru = sV.n

        xb_free = [v_nmuru, v_nmuru]
        t_xq = [t_ffn, t_ffn]
        v_aw_p3 = [v_ffn, v_ffn]
        v_sfree = [v_ffn, v_ffn]
        g_guard = [v_ffn, v_ffn]
        zn_done = [0, 0]
        for s_i in range(SUB):
            par = s_i % 2
            op("sync", lambda e, s_i=s_i, par=par: e.dma_start(
                out=xb[par][:, :], in_=x_shard[s_i * 128:(s_i + 1) * 128, :]),
                waits=[(sV, xb_free[par])], inc=sxb[par], inck=16)
            lx = sxb[par].n
            uw = [(sS, zn_done[par]), (sV, v_aw_p3[par])]
            if s_i >= 2:
                uw.append((sout[par], 16 * (s_i // 2)))
            op("sync", lambda e, s_i=s_i, par=par: e.dma_start(
                out=xm[par][:, :], in_=acc_dram[s_i * 128:(s_i + 1) * 128, :]),
                waits=uw, inc=sxn[par], inck=16)
            lu2 = sxn[par].n
            so(lambda s, s_i=s_i, par=par: s.activation(
                xnf[par][:, :], xm[par][:, :], AF.Identity,
                scale=rstdu[:, s_i:s_i + 1], bias=nmuru[:, s_i:s_i + 1]),
               [(sV, v_nmuru), (sxn[par], lu2)])
            zn_done[par] = sS.n
            s_zn = sS.n
            vo(lambda v, par=par: v.tensor_mul(xnf[par][:, :], xnf[par][:, :],
                                               gb[:, 0:D]),
               [(sS, s_zn)] + ([(sV, v_gb)] if s_i == 0 else []))
            vo(lambda v, par=par: v.tensor_add(xnf[par][:, :], xnf[par][:, :],
                                               gb[:, D:2 * D]))
            v_z = sV.n
            so(lambda s, par=par: s.activation(g_sb0[:, :], xnf[par][:, :], AF.Silu),
               [(sV, v_z), (sV, g_guard[0])])
            s_g = sS.n
            if dbg and s_i == 0:
                op("sync", lambda e: e.dma_start(out=dbg_g, in_=g_sb0[:, :]),
                   waits=[(sS, s_g)], inc=sDg, inck=16)
            op("sync", lambda e, s_i=s_i, par=par: e.dma_start(
                out=xnTs[par][:, :],
                in_=xnT_dram[:, s_i * DB * 128:(s_i + 1) * DB * 128]),
                waits=[(sT, t_xq[par]), (sxT[par], xT_store[par])],
                inc=sxT[par], inck=16)
            l_xT = sxT[par].n
            pbq = 2 + 2 * par
            for jc in range(NJC):
                for b in range(DB):
                    w = []
                    if jc == 0 and b == 0:
                        w = [(sV, v_G_all), (sV, v_sfree[par]), (sxT[par], l_xT)]
                    to(lambda t, jc=jc, b=b, par=par, pbq=pbq: t.matmul(
                        pb[pbq + jc][:, :],
                        xnTs3[par][:, b, 0:128],
                        G_sb[:, b * D + jc * 512:b * D + (jc + 1) * 512],
                        start=(b == 0), stop=(b == DB - 1)),
                       waits=w, last=(jc == NJC - 1 and b == DB - 1))
            t_xq[par] = sT.n
            t_s = sT.n
            for jc in range(NJC):
                vo(lambda v, jc=jc, par=par, pbq=pbq: v.tensor_mul(
                    xm[par][:, jc * 512:(jc + 1) * 512], pb[pbq + jc][:, :],
                    g_sb0[:, jc * 512:(jc + 1) * 512]),
                   [(sT, t_s), (sS, s_g)] + ([(sDg, sDg.n)] if dbg else []))
            v_sfree[par] = sV.n
            g_guard[0] = sV.n
            fw = [(sxb[par], lx)]
            if s_i >= 2:
                fw.append((sout[par], 16 * (s_i // 2)))
            vo(lambda v, par=par: v.tensor_add(xm[par][:, :], xm[par][:, :], xb[par][:, :]), fw)
            xb_free[par] = sV.n
            v_fin = sV.n
            op("sync", lambda e, s_i=s_i, par=par: e.dma_start(
                out=y_d[s_i * 128:(s_i + 1) * 128, :], in_=xm[par][:, :]),
                waits=[(sV, v_fin)], inc=sout[par], inck=16)

        op("sync", lambda e: e.nop(),
           waits=[(sout[0], sout[0].n), (sout[1], sout[1].n)] +
                 ([(sDg, sDg.n)] if dbg else []))

        def emit(e, key):
            for fn, waits, inc, inck in prog[key]:
                for s, v in waits:
                    e.wait_ge(s.h, v)
                inst = fn(e)
                if inc is not None:
                    inst.then_inc(inc.h, inck)

        @block.sync
        def _(e):
            emit(e, "sync")

        @block.gpsimd
        def _(e):
            emit(e, "gpsimd")

        @block.vector
        def _(e):
            emit(e, "vector")

        @block.scalar
        def _(e):
            emit(e, "scalar")

        @block.tensor
        def _(e):
            emit(e, "tensor")

    return nc


def make_inputs(cfg, x2d, Wq, Wk, Wv, Wo, Wr, br, W1, b1, W2, b2, gamma, beta):
    D, E, F, TC = cfg["D"], cfg["E"], cfg["F"], cfg["TC"]
    com = {
        "x_full": np.ascontiguousarray(x2d),
        "WqT": np.ascontiguousarray(Wq.T),
        "Wk": np.ascontiguousarray(Wk),
        "Wv": np.ascontiguousarray(Wv),
        "WoT": np.ascontiguousarray(Wo.T),
        "WrT": np.ascontiguousarray(Wr.T),
        "br": np.ascontiguousarray(br.reshape(1, E)),
        "W1T": np.ascontiguousarray(W1.transpose(0, 2, 1)),
        "b1r": np.ascontiguousarray(b1.reshape(E, F // 128, 128).transpose(0, 2, 1)),
        "W2T": np.ascontiguousarray(W2.transpose(0, 2, 1)),
        "b2": np.ascontiguousarray(b2),
        "gamma": np.ascontiguousarray(gamma.reshape(1, D)),
        "beta": np.ascontiguousarray(beta.reshape(1, D)),
    }
    in_maps = []
    for c in range(NCORES):
        m = dict(com)
        m["x_shard"] = np.ascontiguousarray(x2d[c * TC:(c + 1) * TC])
        in_maps.append(m)
    return in_maps


def kernel(L0_attn_x, L0_attn_Wq, L0_attn_Wk, L0_attn_Wv, L0_attn_Wo,
           L0_ffn_Wr, L0_ffn_br, L0_ffn_W1, L0_ffn_b1, L0_ffn_W2, L0_ffn_b2,
           L0_ffn_gamma, L0_ffn_beta):
    cfg = REAL
    x = np.asarray(L0_attn_x, dtype=np.float32)
    B, S, D = x.shape
    x2d = x.reshape(S, D)
    nc = build_nc(cfg)
    in_maps = make_inputs(
        cfg, x2d,
        np.asarray(L0_attn_Wq, np.float32), np.asarray(L0_attn_Wk, np.float32),
        np.asarray(L0_attn_Wv, np.float32), np.asarray(L0_attn_Wo, np.float32),
        np.asarray(L0_ffn_Wr, np.float32), np.asarray(L0_ffn_br, np.float32),
        np.asarray(L0_ffn_W1, np.float32), np.asarray(L0_ffn_b1, np.float32),
        np.asarray(L0_ffn_W2, np.float32), np.asarray(L0_ffn_b2, np.float32),
        np.asarray(L0_ffn_gamma, np.float32), np.asarray(L0_ffn_beta, np.float32))
    res = run_bass_kernel_spmd(nc, in_maps, core_ids=list(range(NCORES)))
    out = np.concatenate([res.results[c]["y"] for c in range(NCORES)], axis=0)
    return out.reshape(B, S, D)
